# version 16 (final)
# speedup vs baseline: 1.2854x; 1.0016x over previous
import numpy as np
from contextlib import ExitStack

import concourse.bass as bass
import concourse.mybir as mybir
from concourse.alu_op_type import AluOpType as OP
from concourse.bass_utils import run_bass_kernel_spmd
import bass_rust

F32 = mybir.dt.float32
F16 = mybir.dt.float16
BF16 = mybir.dt.bfloat16
I32 = mybir.dt.int32
AF = mybir.ActivationFunctionType
AX = mybir.AxisListType.X

T_TOK = 2048
NT = 16
D = 1024
NH = 16
HD = 64
NP = 8
DH = 4096
C_MAGIC = float(1.5 * 2 ** 23)
LN_EPS = 1e-5
DEBUG_DUMPS = False


def ts(i, n=128):
    return slice(i * n, (i + 1) * n)


class PB:

    def __init__(self, nc, ctx):
        self.nc = nc
        self.ctx = ctx
        self._sems = {}
        self.cnt = {}

    def sem(self, name):
        if name not in self._sems:
            self._sems[name] = self.ctx.enter_context(self.nc.semaphore(name))
            self.cnt[name] = 0
        return self._sems[name]

    def inc(self, instr, name, n=1):
        instr.then_inc(self.sem(name), n)
        self.cnt[name] += n
        return self.cnt[name]

    def tok(self, name):
        self.sem(name)
        return (name, self.cnt[name])

    def wait(self, eng, token):
        name, val = token
        if val > 0:
            eng.wait_ge(self._sems[name], val)

    def wait_now(self, eng, name):
        self.wait(eng, self.tok(name))


def build_program(stage="full"):
    nc = bass.Bass("TRN2", target_bir_lowering=False)

    din = {}

    def dram_in(name, shape, dtype=F32):
        din[name] = nc.dram_tensor(name, list(shape), dtype, kind="ExternalInput").ap()
        return din[name]

    x0_d = dram_in("d_x0", (T_TOK, D))
    wq_d = dram_in("d_wq", (D, D))
    wk_d = dram_in("d_wk", (D, D))
    wv_d = dram_in("d_wv", (D, D))
    wsm_d = dram_in("d_wsm", (D, 80))
    biasrow_d = dram_in("d_biasrow", (1, 80))
    wo_d = dram_in("d_wo", (D, D))
    w1_d = dram_in("d_w1", (DH, D))
    convw_d = dram_in("d_convw", (DH, 3))
    convb_d = dram_in("d_convb", (DH, 1))
    w2_d = dram_in("d_w2", (D, DH))
    iw_d = dram_in("d_iw", (D, D))
    xis_d = dram_in("d_xis", (D, 1))
    gpm_d = dram_in("d_gpm", (D, D))
    sim_d = dram_in("d_sim", (D, 16))
    gates_d = dram_in("d_gates", (1, 16))
    qkvp_d = dram_in("d_qkvp", (D, 3 * NH * HD))
    opj_d = dram_in("d_opj", (D, D))
    ident_d = dram_in("d_identI", (128, 128), BF16)
    negm_d = dram_in("d_negmI", (128, 128))

    out_d = nc.dram_tensor("out", [T_TOK, D], F32, kind="ExternalOutput").ap()
    dbg_d = (nc.dram_tensor("dbg", [T_TOK, DH], F32, kind="Internal").ap()
             if DEBUG_DUMPS else None)
    h2sp_d = nc.dram_tensor("h2spill", [T_TOK, DH], F32, kind="Internal").ap()

    with ExitStack() as ctx:
        e = ctx.enter_context
        p = PB(nc, ctx)
        TEN, VEC, SCA, GPS, SYN = nc.tensor, nc.vector, nc.scalar, nc.gpsimd, nc.sync

        ident = e(nc.sbuf_tensor("ident", [128, 128], BF16))
        negm = e(nc.sbuf_tensor("negm", [128, 128], F32))
        onecol = e(nc.sbuf_tensor("onecol", [128, 1], BF16))
        onecolh = e(nc.sbuf_tensor("onecolh", [128, 1], F16))
        pat = e(nc.sbuf_tensor("pat", [1, 256], BF16))
        x1 = e(nc.sbuf_tensor("x1", [128, NT, D], F32))
        qbs = e(nc.sbuf_tensor("qbs", [128, NT, 1024], F16))
        mu0 = e(nc.sbuf_tensor("mu0", [128, NT], F32))
        isd0 = e(nc.sbuf_tensor("isd0", [128, NT], F32))
        sd0 = e(nc.sbuf_tensor("sd0", [128, NT], F32))
        s1q = e(nc.sbuf_tensor("s1q", [128, NT], F32))
        is1q = e(nc.sbuf_tensor("is1q", [128, NT], F32))
        var0 = e(nc.sbuf_tensor("var0", [128, NT], F32))
        stat6 = e(nc.sbuf_tensor("stat6", [128, 12], F32))
        stat2 = e(nc.sbuf_tensor("stat2", [128, 2], F32))
        mxmx = e(nc.sbuf_tensor("mxmx", [128, 4], F32))
        cpool = e(nc.sbuf_tensor("cpool", [128, 8], F32))
        identf = e(nc.sbuf_tensor("identf", [128, 128], F32))
        identh = e(nc.sbuf_tensor("identh", [128, 128], F16))

        p.inc(SYN.dma_start(ident[:, :], ident_d), "hw", 16)
        p.inc(SYN.dma_start(negm[:, :], negm_d), "hw", 16)
        ld0 = p.tok("hw")
        p.wait(VEC, ld0)
        VEC.memset(cpool[:, 0:1], LN_EPS)
        VEC.memset(cpool[:, 1:2], -2.5)
        VEC.memset(cpool[:, 2:3], C_MAGIC)
        VEC.tensor_copy(identf[:, :], ident[:, :])
        VEC.drain()
        VEC.tensor_copy(identh[:, :], ident[:, :])
        VEC.drain()
        VEC.memset(onecol[:, :], 1.0)
        VEC.memset(onecolh[:, :], 1.0)
        VEC.memset(pat[:, 0:64], 1.0)
        VEC.memset(pat[:, 64:192], 0.0)
        mi = VEC.memset(pat[:, 192:256], 1.0)
        p.inc(mi, "V")
        consts_ready = p.tok("V")

        psA = e(nc.psum_tensor("psA", [128, 2048], F32))
        psB = e(nc.psum_tensor("psB", [128, 1024], F32))
        psC = e(nc.psum_tensor("psC", [128, 1024], F32))

        with ExitStack() as l0ctx:
            e0 = l0ctx.enter_context
            wkvq = e0(nc.sbuf_tensor("wkvq", [128, 8, 3072], BF16))
            wsm = e0(nc.sbuf_tensor("wsm", [128, 8, 80], BF16))
            wo = e0(nc.sbuf_tensor("wo", [128, 8, D], BF16))
            biastile = e0(nc.sbuf_tensor("biastile", [128, 80], F32))
            biasrow = e0(nc.sbuf_tensor("biasrow", [1, 80], BF16))
            state = e0(nc.sbuf_tensor("state", [128, NP, 128], BF16))
            xntm = e0(nc.sbuf_tensor("xntm", [128, D], BF16))
            xnfm = e0(nc.sbuf_tensor("xnfm", [128, 8, 128], BF16))
            kqvs = e0(nc.sbuf_tensor("kqvs", [128, 3072], BF16))
            kks = e0(nc.sbuf_tensor("kks", [128, 2048], BF16))
            vtm = e0(nc.sbuf_tensor("vtm", [128, 1024], BF16))
            qftm = e0(nc.sbuf_tensor("qftm", [128, 1024], BF16))
            qstm = e0(nc.sbuf_tensor("qstm", [128, 1024], BF16))
            qffm = e0(nc.sbuf_tensor("qffm", [128, 8, 128], BF16))
            qsfm = e0(nc.sbuf_tensor("qsfm", [128, 8, 128], BF16))
            otm = e0(nc.sbuf_tensor("otm", [128, 1024], BF16))
            ofm = e0(nc.sbuf_tensor("ofm", [128, 8, 128], BF16))
            smf = e0(nc.sbuf_tensor("smf", [128, 80], F32))
            smt = e0(nc.sbuf_tensor("smt", [128, 80], F32))
            pprod = e0(nc.sbuf_tensor("pprod", [128, 2, 1024], BF16))
            psn = e0(nc.sbuf_tensor("psn", [128, 48], F32))
            sqn = e0(nc.sbuf_tensor("sqn", [128, 32], F32))
            psis = e0(nc.sbuf_tensor("psis", [128, 16], F32))
            surs = e0(nc.sbuf_tensor("surs", [128, 16], F32))
            scr16 = e0(nc.sbuf_tensor("scr16", [128, 16], F32))
            scr16b = e0(nc.sbuf_tensor("scr16b", [128, 16], F32))
            scr16c = e0(nc.sbuf_tensor("scr16c", [128, 16], F32))
            babf = e0(nc.sbuf_tensor("babf", [128, 64], BF16))
            dmx = e0(nc.sbuf_tensor("dmx", [128, 32], BF16))
            dmr = e0(nc.sbuf_tensor("dmr", [1, 32], F32))
            dmr2 = e0(nc.sbuf_tensor("dmr2", [1, 32], F32))
            dmb = e0(nc.sbuf_tensor("dmb", [1, 32], BF16))
            svf = e0(nc.sbuf_tensor("svf", [128, 8], F32))
            svs = e0(nc.sbuf_tensor("svs", [128, 8], F32))
            qx = e0(nc.sbuf_tensor("qx", [128, 512], F32))
            qyL = e0(nc.sbuf_tensor("qyL", [128, 512], F32))
            qyiL = e0(nc.sbuf_tensor("qyiL", [128, 512], I32))

            for k in range(8):
                p.inc(GPS.dma_start(wkvq[:, k, 0:1024], wk_d[ts(k), :]), "gw_kvq", 16)
                p.inc(GPS.dma_start(wkvq[:, k, 1024:2048], wv_d[ts(k), :]), "gw_kvq", 16)
                p.inc(GPS.dma_start(wkvq[:, k, 2048:3072], wq_d[ts(k), :]), "gw_kvq", 16)
                p.inc(GPS.dma_start(wsm[:, k, :], wsm_d[ts(k), :]), "gw_kvq", 16)
            wkvq_ready = p.tok("gw_kvq")
            for k in range(8):
                p.inc(GPS.dma_start(wo[:, k, :], wo_d[ts(k), :]), "gw_wo", 16)
            p.inc(GPS.dma_start(biasrow[:, :], biasrow_d), "gw_wo", 16)
            wo_ready = p.tok("gw_wo")

            p.wait(TEN, consts_ready)
            p.wait(TEN, wo_ready)
            mi = TEN.matmul(psC[:, 900:980], pat[0:1, 0:128], biasrow[0:1, :],
                            start=True, stop=True)
            p.inc(mi, "T")
            p.wait(VEC, p.tok("T"))
            mi = VEC.tensor_copy(biastile[:, :], psC[:, 900:980])
            p.inc(mi, "V")
            bias_ready = p.tok("V")

            mi = VEC.memset(state[:, :, :], 0.0)
            p.inc(mi, "V")

            for tt0 in range(NT):
                p.inc(SYN.dma_start(x1[:, tt0, :], x0_d[ts(tt0), :]),
                      "x%d" % tt0, 16)

            evac_done = [None] * NT
            prev = {}

            stats_tok = [None] * NT
            sd_tok = [None] * NT
            mxbuf = [None] * NT

            def emit_stats(tt):
                p.wait(VEC, ("x%d" % tt, 16))
                xtt = x1[:, tt, :]
                VEC.bn_stats(stat6[:, 0:6], xtt[:, 0:512])
                mi = VEC.bn_stats(stat6[:, 6:12], xtt[:, 512:1024])
                VEC.drain()
                mi = VEC.bn_aggr(stat2[:, :], stat6[:, :])
                VEC.drain()
                VEC.tensor_copy(mu0[:, tt:tt + 1], stat2[:, 0:1])
                VEC.tensor_copy(var0[:, tt:tt + 1], stat2[:, 1:2])
                VEC.drain()
                mi = VEC.tensor_reduce(mxmx[:, 2 * (tt % 2):2 * (tt % 2) + 1],
                                       xtt, AX, OP.max)
                VEC.drain()
                mi = VEC.tensor_reduce(mxmx[:, 2 * (tt % 2) + 1:
                                            2 * (tt % 2) + 2],
                                       xtt, AX, OP.min)
                p.inc(mi, "V")
                stats_tok[tt] = p.tok("V")

            emit_stats(0)
            p.wait(SCA, stats_tok[0])
            mi = SCA.activation(sd0[:, 0:1], var0[:, 0:1], AF.Sqrt,
                                bias=cpool[:, 0:1])
            p.inc(mi, "S")
            sd_tok[0] = p.tok("S")

            trxn = [None] * NT
            xnfm_v = [None] * NT
            xnfm_s = [None] * NT
            kqv_tok = [None] * NT
            smallsE = [None] * NT
            siluT = [None] * NT
            qgb = [None] * NT
            kvb = [None] * NT
            psiS = [None] * NT
            g128 = [None] * NT
            svr = [None] * NT
            qtrd = [None] * NT
            qtr_v = [None] * NT
            qtr_s = [None] * NT
            udone = [None] * NT
            sttok = [None] * NT
            ommT = [None] * NT
            otm_v = [None] * NT
            otm_s = [None] * NT
            troT = [None] * NT
            ofm_v = [None] * NT
            ofm_s = [None] * NT
            wokT = [None] * NT
            dmxT = [None] * NT
            svTt = [None] * NT

            def emit_Btail_head(a):
                p.wait(TEN, smallsE[a + 1])
                p.wait(TEN, sttok[a])
                for hh in range(2):
                    for pp in range(NP):
                        cols = slice(hh * 512 + pp * 64,
                                     hh * 512 + (pp + 1) * 64)
                        mi = TEN.matmul(psC[:, cols],
                                        qffm[ts(hh, 64), pp, :],
                                        state[ts(hh, 64), pp, 0:64],
                                        start=True, stop=False,
                                        tile_position=(hh * 64, 0))
                        mi = TEN.matmul(psC[:, cols],
                                        qsfm[ts(hh, 64), pp, :],
                                        state[ts(hh, 64), pp, 64:128],
                                        start=False, stop=True,
                                        tile_position=(hh * 64, 0))
                p.inc(mi, "T")
                ommT[a] = p.tok("T")

            def emit_otm_evacs(a):
                p.wait(SCA, ommT[a])
                if a > 0:
                    p.wait(SCA, troT[a - 1])
                mi = SCA.copy(
                    bass.AP(otm, 0, [[1024, 128], [128, 8], [1, 64]]),
                    psC[:, 0:512])
                p.inc(mi, "S")
                otm_s[a] = p.tok("S")
                mi = SCA.copy(
                    bass.AP(otm, 64, [[1024, 128], [128, 8], [1, 64]]),
                    psC[:, 512:1024])
                SCA.drain()
                p.inc(mi, "S")
                otm_v[a] = p.tok("S")

            def emit_otr_wo(a, silu_tok):
                if silu_tok is not None:
                    p.wait(TEN, silu_tok)
                p.wait(TEN, otm_v[a])
                p.wait(TEN, otm_s[a])
                for half in range(2):
                    for j in range(4):
                        ech = half * 4 + j
                        mi = TEN.matmul(
                            psA[:, 1024 + half * 512 + j * 128:
                                1024 + half * 512 + (j + 1) * 128],
                            otm[:, ts(ech)], ident[:, :],
                            start=True, stop=True)
                    p.inc(mi, "T")
                troT[a] = p.tok("T")
                p.wait(SCA, troT[a])
                if a > 0:
                    p.wait(SCA, wokT[a - 1])
                mi = SCA.copy(
                    bass.AP(ofm, 0, [[1024, 128], [1, 512]]),
                    psA[:, 1024:1536])
                p.inc(mi, "S")
                ofm_s[a] = p.tok("S")
                p.wait(VEC, troT[a])
                if a > 0:
                    p.wait(VEC, wokT[a - 1])
                mi = VEC.tensor_copy(
                    bass.AP(ofm, 4 * 128, [[1024, 128], [1, 512]]),
                    psA[:, 1536:2048])
                p.inc(mi, "V")
                ofm_v[a] = p.tok("V")
                p.wait(TEN, ofm_v[a])
                p.wait(TEN, ofm_s[a])
                for ech in range(8):
                    for n in range(2):
                        mi = TEN.matmul(psB[:, n * 512:(n + 1) * 512],
                                        ofm[:, ech, :],
                                        wo[:, ech, n * 512:(n + 1) * 512],
                                        start=(ech == 0), stop=(ech == 7))
                p.inc(mi, "T")
                wokT[a] = p.tok("T")

            def emit_x1(a):
                p.wait(VEC, wokT[a])
                mi = VEC.tensor_tensor(x1[:, a, :], x1[:, a, :],
                                       psB[:, 0:1024], OP.add)
                p.inc(mi, "V")
                evac_done[a] = p.tok("V")

            xn_ready_t = [None] * NT

            def emit_B1(tt):
                xtt = x1[:, tt, :]
                p.wait(VEC, ("x%d" % tt, 16))
                mxmxv = mxmx[:, 2 * (tt % 2):2 * (tt % 2) + 2]
                p.wait(VEC, sd_tok[tt])
                mi = VEC.reciprocal(isd0[:, tt:tt + 1], sd0[:, tt:tt + 1])
                VEC.drain()
                VEC.scalar_tensor_tensor(scr16[:, 0:1], mu0[:, tt:tt + 1],
                                         -1.0, mxmxv[:, 0:1], OP.mult, OP.add)
                VEC.scalar_tensor_tensor(scr16[:, 1:2], mu0[:, tt:tt + 1],
                                         1.0, mxmxv[:, 1:2], OP.mult,
                                         OP.subtract)
                VEC.drain()
                VEC.tensor_tensor(scr16[:, 2:3], scr16[:, 0:1], scr16[:, 1:2],
                                  OP.max)
                VEC.drain()
                VEC.tensor_tensor(scr16[:, 3:4], scr16[:, 2:3],
                                  isd0[:, tt:tt + 1], OP.mult)
                VEC.drain()
                VEC.tensor_scalar(scr16[:, 4:5], scr16[:, 3:4], 1e-5, None,
                                  OP.max)
                VEC.drain()
                mi = VEC.reciprocal(scr16[:, 5:6], scr16[:, 4:5])
                VEC.drain()
                VEC.tensor_scalar(s1q[:, tt:tt + 1], scr16[:, 5:6], 127.0,
                                  None, OP.mult)
                VEC.drain()
                mi = VEC.reciprocal(is1q[:, tt:tt + 1], s1q[:, tt:tt + 1])
                VEC.drain()
                VEC.tensor_tensor(scr16[:, 6:7], mu0[:, tt:tt + 1],
                                  isd0[:, tt:tt + 1], OP.mult)
                VEC.drain()
                mi = VEC.tensor_scalar(scr16[:, 7:8], scr16[:, 6:7], -1.0,
                                       None, OP.mult)
                VEC.drain()
                if tt > 0:
                    p.wait(VEC, trxn[tt - 1])
                mi = VEC.tensor_scalar(xntm[:, :], xtt, isd0[:, tt:tt + 1],
                                       scr16[:, 7:8], OP.mult, OP.add)
                p.inc(mi, "V")
                xn_ready_t[tt] = p.tok("V")

            emit_B1(0)

            for t in range(NT):
                tb = t % 2
                xt = x1[:, t, :]

                p.wait(TEN, xn_ready_t[t])
                if t > 0:
                    p.wait(TEN, svr[t - 1])
                for half in range(2):
                    for j in range(4):
                        kch = half * 4 + j
                        mi = TEN.matmul(psB[:, half * 512 + j * 128:
                                            half * 512 + (j + 1) * 128],
                                        xntm[:, ts(kch)], ident[:, :],
                                        start=True, stop=True)
                    p.inc(mi, "T")
                trxn[t] = p.tok("T")
                p.wait(SCA, trxn[t])
                mi = SCA.copy(
                    bass.AP(xnfm, 0, [[1024, 128], [1, 512]]),
                    psB[:, 0:512])
                p.inc(mi, "S")
                xnfm_s[t] = p.tok("S")
                p.wait(VEC, trxn[t])
                mi = VEC.tensor_copy(
                    bass.AP(xnfm, 512, [[1024, 128], [1, 512]]),
                    psB[:, 512:1024])
                p.inc(mi, "V")
                xnfm_v[t] = p.tok("V")

                VEC.tensor_tensor(scr16[:, 8:9], isd0[:, t:t + 1],
                                  s1q[:, t:t + 1], OP.mult)
                VEC.drain()
                VEC.tensor_tensor(scr16[:, 9:10], mu0[:, t:t + 1],
                                  scr16[:, 8:9], OP.mult)
                VEC.drain()
                VEC.tensor_scalar(scr16[:, 10:11], scr16[:, 9:10], -1.0,
                                  None, OP.mult)
                VEC.drain()
                for qh in range(2):
                    xsl = xt[:, qh * 512:(qh + 1) * 512]
                    mi = VEC.tensor_scalar(qx[:, :], xsl, scr16[:, 8:9],
                                           scr16[:, 10:11], OP.mult, OP.add)
                    VEC.drain()
                    VEC.tensor_scalar(qyL[:, :], qx[:, :], C_MAGIC, None,
                                      OP.add)
                    VEC.drain()
                    VEC.tensor_copy(qyiL[:, :], qyL[:, :])
                    VEC.drain()
                    VEC.tensor_tensor(qyL[:, :], qyL[:, :], qyiL[:, :],
                                      OP.subtract)
                    VEC.drain()
                    VEC.tensor_scalar(qx[:, :], qyL[:, :], 0.5, None,
                                      OP.is_ge)
                    VEC.drain()
                    VEC.scalar_tensor_tensor(qx[:, :], qyL[:, :], -0.5,
                                             qx[:, :], OP.is_gt, OP.add)
                    VEC.drain()
                    VEC.scalar_tensor_tensor(
                        qbs[:, t, qh * 512:(qh + 1) * 512], qyiL[:, :],
                        C_MAGIC + 1.0, qx[:, :], OP.subtract, OP.add)
                    VEC.drain()

                p.wait(TEN, wkvq_ready)
                p.wait(TEN, xnfm_v[t])
                p.wait(TEN, xnfm_s[t])
                if t > 0:
                    p.wait(TEN, sttok[t - 1])
                    p.wait(TEN, qtr_v[t - 1])
                    p.wait(TEN, qtr_s[t - 1])
                if t > 1:
                    p.wait(TEN, otm_v[t - 2])
                    p.wait(TEN, otm_s[t - 2])
                for k in range(8):
                    lhs = xnfm[:, k, :]
                    for n in range(4):
                        mi = TEN.matmul(psA[:, n * 512:(n + 1) * 512], lhs,
                                        wkvq[:, k, n * 512:(n + 1) * 512],
                                        start=(k == 0), stop=(k == 7))
                    for n in range(2):
                        mi = TEN.matmul(psB[:, n * 512:(n + 1) * 512], lhs,
                                        wkvq[:, k, 2048 + n * 512:
                                             2048 + (n + 1) * 512],
                                        start=(k == 0), stop=(k == 7))
                    mi = TEN.matmul(psC[:, 0:80], lhs, wsm[:, k, :],
                                    start=(k == 0), stop=(k == 7))
                p.inc(mi, "T")
                kqv_tok[t] = p.tok("T")

                p.wait(VEC, kqv_tok[t])
                p.wait(VEC, bias_ready)
                mi = VEC.tensor_tensor(smt[:, :], psC[:, 0:80],
                                       biastile[:, :], OP.add)
                VEC.drain()
                p.inc(mi, "V")
                smallsE[t] = p.tok("V")

                if t > 0:
                    emit_Btail_head(t - 1)

                p.wait(SCA, kqv_tok[t])
                if t > 0:
                    p.wait(SCA, qgb[t - 1])
                SCA.activation(kqvs[:, 0:1024], psA[:, 0:1024], AF.Silu)
                SCA.activation(kqvs[:, 1024:2048], psA[:, 1024:2048], AF.Silu)
                mi = SCA.activation(kqvs[:, 2048:3072], psB[:, 0:1024],
                                    AF.Silu)
                p.inc(mi, "S")
                siluT[t] = p.tok("S")
                p.wait(SCA, smallsE[t])
                mi = SCA.activation(smt[:, :], smt[:, :], AF.Tanh, scale=0.5)
                p.inc(mi, "S")
                smt_tok = p.tok("S")
                p.wait(VEC, smt_tok)
                mi = VEC.tensor_scalar(smf[:, :], smt[:, :], 0.5, 0.5,
                                       OP.mult, OP.add)
                VEC.drain()
                mi = VEC.tensor_copy(babf[:, 0:16], smf[:, 0:16])
                VEC.drain()

                p.wait(VEC, siluT[t])
                if t > 0:
                    p.wait(VEC, udone[t - 1])
                bbeta = bass.AP(babf, 0, [[64, 128], [1, 16], [0, 64]])
                mi = VEC.tensor_tensor(
                    bass.AP(kks, 0, [[2048, 128], [128, 16], [1, 64]]),
                    kqvs[:, 0:1024], bbeta, OP.mult)
                VEC.drain()
                mi = VEC.tensor_tensor(vtm[:, :], kqvs[:, 1024:2048], bbeta,
                                       OP.mult)
                VEC.drain()
                p.inc(mi, "V")
                kvb[t] = p.tok("V")

                if t > 0:
                    emit_otm_evacs(t - 1)

                if t + 1 < NT:
                    emit_stats(t + 1)

                kev = bass.AP(kks, 0, [[2048, 128], [128, 16], [1, 64]])
                v3 = bass.AP(vtm, 0, [[1024, 128], [64, 16], [1, 64]])
                p.wait(GPS, kvb[t])
                if t > 0:
                    p.wait(GPS, psiS[t - 1])
                gtoks = []
                for sl, (a1, a2) in enumerate(((kev, v3), (kev, kev),
                                               (v3, v3))):
                    if sl >= 2:
                        p.wait(GPS, ("V", p.cnt["V"]))
                    mi = GPS.tensor_tensor(pprod[:, sl % 2, :], a1, a2,
                                           OP.mult)
                    GPS.drain()
                    p.inc(mi, "G")
                    gtoks.append(p.tok("G"))
                    if sl == 0:
                        if t > 0:
                            emit_otr_wo(t - 1, siluT[t])
                    if sl < 2:
                        p.wait(VEC, gtoks[sl])
                        pr3 = bass.AP(pprod, (sl % 2) * 1024,
                                      [[2048, 128], [64, 16], [1, 64]])
                        mi = VEC.tensor_reduce(
                            psn[:, sl * 16:(sl + 1) * 16], pr3, AX, OP.add)
                        VEC.drain()
                        p.inc(mi, "V")
                p.wait(VEC, gtoks[2])
                mi = VEC.tensor_reduce(
                    psn[:, 32:48],
                    bass.AP(pprod, 0, [[2048, 128], [64, 16], [1, 64]]),
                    AX, OP.add)
                VEC.drain()
                p.inc(mi, "V")
                psiS[t] = p.tok("V")

                p.wait(SCA, psiS[t])
                mi = SCA.activation(sqn[:, 0:32], psn[:, 16:48], AF.Sqrt)
                if t + 1 < NT:
                    p.wait(SCA, stats_tok[t + 1])
                    mi = SCA.activation(sd0[:, t + 1:t + 2],
                                        var0[:, t + 1:t + 2], AF.Sqrt,
                                        bias=cpool[:, 0:1])
                mi = SCA.activation(psn[:, 0:16], psn[:, 0:16], AF.Abs)
                p.inc(mi, "S")
                sq_done = p.tok("S")
                if t + 1 < NT:
                    sd_tok[t + 1] = sq_done

                p.wait(VEC, sq_done)
                VEC.tensor_scalar(sqn[:, 0:32], sqn[:, 0:32], 1e-8, None,
                                  OP.add)
                VEC.drain()
                VEC.tensor_tensor(scr16[:, :], sqn[:, 0:16], sqn[:, 16:32],
                                  OP.mult)
                VEC.drain()
                mi = VEC.reciprocal(scr16b[:, :], scr16[:, :])
                VEC.drain()
                mi = VEC.tensor_tensor(psis[:, :], psn[:, 0:16],
                                       scr16b[:, :], OP.mult)
                p.inc(mi, "V")
                p.wait(SCA, p.tok("V"))
                mi = SCA.activation(scr16[:, :], psis[:, :], AF.Tanh,
                                    scale=1.5)
                SCA.drain()
                p.inc(mi, "S")
                t1_tok = p.tok("S")
                mi = SCA.activation(scr16c[:, :], scr16[:, :], AF.Tanh,
                                    scale=2.5)
                p.inc(mi, "S")
                t2_tok = p.tok("S")
                p.wait(VEC, t1_tok)
                mi = VEC.tensor_scalar(psis[:, :], scr16[:, :], 0.5, 0.5,
                                       OP.mult, OP.add)
                VEC.drain()
                VEC.tensor_scalar(psn[:, 0:16], psis[:, :], -0.1, 1.0,
                                  OP.mult, OP.add)
                VEC.tensor_scalar(scr16b[:, :], psis[:, :], -0.05, 1.0,
                                  OP.mult, OP.add)
                VEC.drain()
                if t > 0:
                    p.wait(VEC, dmxT[t - 1])
                VEC.tensor_tensor(dmx[:, 0:16], smf[:, 16:32], psn[:, 0:16],
                                  OP.mult)
                mi = VEC.tensor_tensor(dmx[:, 16:32], smf[:, 32:48],
                                       scr16b[:, :], OP.mult)
                VEC.drain()
                p.inc(mi, "V")
                dmx_ready = p.tok("V")
                p.wait(VEC, t2_tok)
                mi = VEC.tensor_scalar(surs[:, :], scr16c[:, :], 0.5, 0.5,
                                       OP.mult, OP.add)
                VEC.drain()
                mi = VEC.tensor_tensor(babf[:, 16:32], surs[:, :], surs[:, :],
                                       OP.mult)
                VEC.drain()
                VEC.tensor_scalar(scr16[:, :], psis[:, :], 0.3, 0.5, OP.mult,
                                  OP.add)
                VEC.tensor_scalar(scr16b[:, :], psis[:, :], -0.3, 0.5,
                                  OP.mult, OP.add)
                VEC.drain()
                VEC.tensor_tensor(babf[:, 32:48], scr16[:, :], smf[:, 48:64],
                                  OP.mult)
                VEC.tensor_tensor(babf[:, 48:64], scr16b[:, :], smf[:, 64:80],
                                  OP.mult)
                VEC.drain()
                bsur2 = bass.AP(babf, 16, [[64, 128], [1, 16], [0, 64]])
                mi = VEC.tensor_tensor(
                    bass.AP(kks, 64, [[2048, 128], [128, 16], [1, 64]]),
                    kev, bsur2, OP.mult)
                VEC.drain()
                bgf = bass.AP(babf, 32, [[64, 128], [1, 16], [0, 64]])
                bgs = bass.AP(babf, 48, [[64, 128], [1, 16], [0, 64]])
                if t > 0:
                    p.wait(VEC, qtrd[t - 1])
                VEC.tensor_tensor(qftm[:, :], kqvs[:, 2048:3072], bgf,
                                  OP.mult)
                mi = VEC.tensor_tensor(qstm[:, :], kqvs[:, 2048:3072], bgs,
                                       OP.mult)
                p.inc(mi, "V")
                qgb[t] = p.tok("V")

                if t > 0:
                    emit_x1(t - 1)

                if t + 1 < NT:
                    emit_B1(t + 1)

                p.wait(TEN, dmx_ready)
                if t > 0:
                    p.wait(TEN, evac_done[t - 1])
                mi = TEN.matmul(psB[0:1, 0:32], onecol[:, :], dmx[:, :],
                                start=True, stop=True)
                p.inc(mi, "T")
                dmxT[t] = p.tok("T")
                p.wait(VEC, dmxT[t])
                if t > 0:
                    p.wait(VEC, g128[t - 1])
                mi = VEC.tensor_scalar(dmr[:, :], psB[0:1, 0:32], 1.0 / 128.0,
                                       None, OP.mult)
                VEC.drain()
                p.inc(mi, "V")
                p.wait(GPS, p.tok("V"))
                if t > 0:
                    p.wait(GPS, svTt[t - 1])
                for _ in range(3):
                    GPS.tensor_tensor(dmr2[:, :], dmr[:, :], dmr[:, :],
                                      OP.mult)
                    GPS.drain()
                    GPS.tensor_tensor(dmr[:, :], dmr2[:, :], dmr2[:, :],
                                      OP.mult)
                    GPS.drain()
                GPS.tensor_tensor(dmr2[:, :], dmr[:, :], dmr[:, :], OP.mult)
                GPS.drain()
                mi = GPS.tensor_copy(dmb[:, :], dmr2[:, :])
                GPS.drain()
                p.inc(mi, "G")
                g128[t] = p.tok("G")
                p.wait(TEN, g128[t])
                mi = TEN.matmul(psB[:, 32:40], pat[0:1, 0:128],
                                bass.AP(dmb, 0, [[32, 1], [2, 8]]),
                                start=True, stop=False)
                mi = TEN.matmul(psB[:, 32:40], pat[0:1, 128:256],
                                bass.AP(dmb, 1, [[32, 1], [2, 8]]),
                                start=False, stop=True)
                mi = TEN.matmul(psB[:, 40:48], pat[0:1, 0:128],
                                bass.AP(dmb, 16, [[32, 1], [2, 8]]),
                                start=True, stop=False)
                mi = TEN.matmul(psB[:, 40:48], pat[0:1, 128:256],
                                bass.AP(dmb, 17, [[32, 1], [2, 8]]),
                                start=False, stop=True)
                p.inc(mi, "T")
                sv_T = p.tok("T")
                svTt[t] = sv_T
                p.wait(VEC, sv_T)
                VEC.tensor_copy(svf[:, :], psB[:, 32:40])
                mi = VEC.tensor_copy(svs[:, :], psB[:, 40:48])
                VEC.drain()
                p.inc(mi, "V")
                svr[t] = p.tok("V")

                p.wait(TEN, qgb[t])
                if t > 0:
                    p.wait(TEN, ofm_v[t - 1])
                    p.wait(TEN, ofm_s[t - 1])
                for srct, dstt in ((qftm, qffm), (qstm, qsfm)):
                    for half in range(2):
                        for j in range(4):
                            kch = half * 4 + j
                            mi = TEN.matmul(
                                psA[:, 1024 + half * 512 + j * 128:
                                    1024 + half * 512 + (j + 1) * 128],
                                srct[:, ts(kch)], ident[:, :],
                                start=True, stop=True)
                        p.inc(mi, "T")
                    tr_tok = p.tok("T")
                    p.wait(SCA, tr_tok)
                    if t > 0:
                        p.wait(SCA, ommT[t - 1])
                    mi = SCA.copy(
                        bass.AP(dstt, 0, [[1024, 128], [1, 512]]),
                        psA[:, 1024:1536])
                    p.inc(mi, "S")
                    p.wait(VEC, tr_tok)
                    if t > 0:
                        p.wait(VEC, ommT[t - 1])
                    mi = VEC.tensor_copy(
                        bass.AP(dstt, 4 * 128, [[1024, 128], [1, 512]]),
                        psA[:, 1536:2048])
                    p.inc(mi, "V")
                    p.wait(TEN, p.tok("V"))
                    p.wait(TEN, p.tok("S"))
                qtrd[t] = p.tok("T")
                qtr_v[t] = p.tok("V")
                qtr_s[t] = p.tok("S")

                p.wait(TEN, kvb[t])
                p.wait(TEN, qgb[t])
                u_toks = []
                for pp in range(NP):
                    for hh in range(2):
                        h = 2 * pp + hh
                        mi = TEN.matmul(
                            psA[ts(hh, 64), pp * 128:(pp + 1) * 128],
                            vtm[:, h * 64:(h + 1) * 64],
                            kks[:, h * 128:(h + 1) * 128],
                            start=True, stop=True,
                            tile_position=(0, hh * 64))
                    if pp == 3 or pp == 7:
                        p.inc(mi, "T")
                        u_toks.append(p.tok("T"))
                udone[t] = u_toks[1]

                p.wait(VEC, u_toks[1])
                p.wait(VEC, svr[t])
                for pp in range(NP):
                    VEC.scalar_tensor_tensor(
                        state[:, pp, 0:64], state[:, pp, 0:64],
                        svf[:, pp:pp + 1],
                        psA[:, pp * 128:pp * 128 + 64], OP.mult, OP.add)
                    mi = VEC.scalar_tensor_tensor(
                        state[:, pp, 64:128], state[:, pp, 64:128],
                        svs[:, pp:pp + 1],
                        psA[:, pp * 128 + 64:(pp + 1) * 128], OP.mult,
                        OP.add)
                VEC.drain()
                p.inc(mi, "V")
                sttok[t] = p.tok("V")

            a = NT - 1
            p.wait(TEN, sttok[a])
            p.wait(TEN, otm_v[a - 1])
            p.wait(TEN, otm_s[a - 1])
            for hh in range(2):
                for pp in range(NP):
                    cols = slice(hh * 512 + pp * 64, hh * 512 + (pp + 1) * 64)
                    mi = TEN.matmul(psC[:, cols], qffm[ts(hh, 64), pp, :],
                                    state[ts(hh, 64), pp, 0:64],
                                    start=True, stop=False,
                                    tile_position=(hh * 64, 0))
                    mi = TEN.matmul(psC[:, cols], qsfm[ts(hh, 64), pp, :],
                                    state[ts(hh, 64), pp, 64:128],
                                    start=False, stop=True,
                                    tile_position=(hh * 64, 0))
            p.inc(mi, "T")
            ommT[a] = p.tok("T")
            emit_otm_evacs(a)
            p.wait(TEN, qtr_v[a])
            p.wait(TEN, qtr_s[a])
            emit_otr_wo(a, None)
            emit_x1(a)
            l0_done = p.tok("V")

        if stage == "l0a":
            p.wait(SYN, l0_done)
            for t in range(NT):
                p.inc(SYN.dma_start(out_d[ts(t), :], x1[:, t, :]), "out", 16)
            p.wait_now(SYN, "out")
            return nc, din

        build_ffn(nc, p, ctx, din, x1, psA, psB, psC, ident, identf, onecol,
                  pat, mu0, isd0, is1q, qbs, w1_d, convw_d, convb_d, w2_d,
                  h2sp_d, stage, dbg_d, identh)
        if stage == "l0":
            p.wait_now(SYN, "hw")
            p.wait(SYN, p.tok("V"))
            for t in range(NT):
                p.inc(SYN.dma_start(out_d[ts(t), :], x1[:, t, :]), "out", 16)
            p.wait_now(SYN, "out")
            return nc, din

        build_l1(nc, p, ctx, din, x1, qbs, psA, psB, psC, ident, identf,
                 identh, onecolh, pat, negm, cpool, iw_d, xis_d, gpm_d, sim_d,
                 gates_d, qkvp_d, opj_d, out_d)
        p.wait_now(SYN, "out")
        return nc, din


def build_ffn(nc, p, ctx, din, x1, psA, psB, psC, ident, identf, onecol, pat,
              mu0, isd0, is1q, qbs, w1_d, convw_d, convb_d, w2_d, h2sp_d,
              stage, dbg_d=None, identh=None):
    TEN, VEC, SCA, GPS, SYN = nc.tensor, nc.vector, nc.scalar, nc.gpsimd, nc.sync
    e = ctx.enter_context

    absc2w = e(nc.sbuf_tensor("absc2w", [128, 32], F32))
    ffn_start = p.tok("V")
    for eng in (TEN, SCA, GPS, SYN):
        p.wait(eng, ffn_start)

    with ExitStack() as fctx:
        ef = fctx.enter_context
        w1T = ef(nc.sbuf_tensor("w1T", [128, 8, 4096], F16))
        absac = ef(nc.sbuf_tensor("absac", [128, 36], F32))
        wsc = ef(nc.sbuf_tensor("wsc", [128, 8], F32))
        tcol = ef(nc.sbuf_tensor("tcol", [128, 8], F32))
        cvw = ef(nc.sbuf_tensor("cvw", [128, 32, 3], F32))
        cvb = ef(nc.sbuf_tensor("cvb", [128, 32], F32))
        wst = ef(nc.sbuf_tensor("wst", [128, 2, 1024], F32))
        wstm = ef(nc.sbuf_tensor("wstm", [128, 2, 1024], F32))
        ternb = ef(nc.sbuf_tensor("ternb", [128, 1024], F16))
        m1b = ef(nc.sbuf_tensor("m1b", [128, 1024], F16))
        m2b = ef(nc.sbuf_tensor("m2b", [128, 1024], F16))
        sA = fctx.enter_context(ExitStack())
        es = sA.enter_context
        qy = es(nc.sbuf_tensor("qy", [128, 1024], F32))
        diagq = es(nc.sbuf_tensor("diagq", [128, 128], F16))

        p.inc(GPS.dma_start(cvw[:, :, :], bass.AP(convw_d.tensor, 0, [[3, 128], [384, 32], [1, 3]])), "cvwg", 16)
        p.inc(GPS.dma_start(cvb[:, :], bass.AP(convb_d.tensor, 0, [[1, 128], [128, 32]]), allow_slow_non_contiguous=True), "cvwg", 16)
        cvw_tok = p.tok("cvwg")

        nw1 = 32
        f0_tok = [None] * nw1
        for c in range(nw1):
            if c >= 2:
                p.wait(SYN, f0_tok[c - 2])
            p.inc(SYN.dma_start(wst[:, c % 2, :], w1_d[ts(c), :]),
                  "w1a" if c % 2 == 0 else "w1b", 16)
            p.wait(SCA, ("w1a" if c % 2 == 0 else "w1b", 16 * (c // 2 + 1)))
            mi = SCA.activation(m1b[:, :], wst[:, c % 2, :], AF.Abs,
                                accum_out=absac[:, c:c + 1])
            SCA.drain()
            p.inc(mi, "S")
            f0_tok[c] = p.tok("S")


        f1_tok = [None] * NT
        for t in range(NT):
            mi = VEC.tensor_scalar(diagq[:, :], identh[:, :],
                                   is1q[:, t:t + 1], None, OP.mult)
            VEC.drain()
            p.inc(mi, "V")
            p.wait(TEN, p.tok("V"))
            if t > 0:
                p.wait(TEN, f1_tok[t - 1][0])
                p.wait(TEN, f1_tok[t - 1][1])
            for half in range(2):
                for j in range(4):
                    kch = half * 4 + j
                    mi = TEN.matmul(psC[:, half * 512 + j * 128:
                                        half * 512 + (j + 1) * 128],
                                    qbs[:, t, ts(kch)], diagq[:, :],
                                    start=True, stop=True)
                p.inc(mi, "T")
            qb_free = p.tok("T")
            p.wait(VEC, qb_free)
            VEC.tensor_copy(qbs[:, t, 0:512], psC[:, 0:512])
            mi = VEC.tensor_copy(qbs[:, t, 512:1024], psC[:, 512:1024])
            VEC.drain()
            p.inc(mi, "V")
            f1_tok[t] = (p.tok("V"), p.tok("V"))
        xqs_ready = p.tok("V")
        xqs_ready_s = p.tok("V")

        p.wait(VEC, f0_tok[nw1 - 1])
        mi = VEC.tensor_reduce(absac[:, 33:34], absac[:, 0:32], AX, OP.add)
        VEC.drain()
        p.inc(mi, "V")
        wmb = es(nc.sbuf_tensor("wmb", [128, 4], BF16))
        wrow = es(nc.sbuf_tensor("wrow", [1, 4], BF16))
        mi = VEC.tensor_copy(wmb[:, 0:1], absac[:, 33:34])
        VEC.drain()
        mi = VEC.scalar_tensor_tensor(wmb[:, 1:2], wmb[:, 0:1], -1.0,
                                      absac[:, 33:34], OP.mult, OP.add)
        VEC.drain()
        p.inc(mi, "V")
        p.wait(TEN, p.tok("V"))
        mi = TEN.matmul(psC[0:1, 0:2], onecol[:, :], wmb[:, 0:2],
                        start=True, stop=True)
        p.inc(mi, "T")
        p.wait(VEC, p.tok("T"))
        mi = VEC.tensor_copy(absac[:1, 2:4], psC[0:1, 0:2])
        VEC.drain()
        mi = VEC.tensor_tensor(absac[:1, 3:4], absac[:1, 2:3], absac[:1, 3:4],
                               OP.add)
        VEC.drain()
        mi = VEC.tensor_copy(wrow[0:1, 0:1], absac[:1, 3:4])
        VEC.drain()
        mi = VEC.scalar_tensor_tensor(wrow[0:1, 1:2], wrow[0:1, 0:1], -1.0,
                                      absac[:1, 3:4], OP.mult, OP.add)
        VEC.drain()
        p.inc(mi, "V")
        p.wait(TEN, p.tok("V"))
        for ii, pslice in enumerate((pat[0:1, 0:128], pat[0:1, 128:256])):
            mi = TEN.matmul(psC[:, 1:3], pslice, wrow[0:1, 0:2],
                            start=(ii == 0), stop=True,
                            skip_group_check=True)
        p.inc(mi, "T")
        p.wait(VEC, p.tok("T"))
        mi = VEC.tensor_copy(wsc[:, 6:8], psC[:, 1:3])
        VEC.drain()
        mi = VEC.tensor_tensor(wsc[:, 0:1], wsc[:, 6:7], wsc[:, 7:8], OP.add)
        VEC.drain()
        VEC.tensor_scalar(wsc[:, 1:2], wsc[:, 0:1], 1.0 / (DH * D), None,
                          OP.mult)
        VEC.drain()
        VEC.tensor_scalar(wsc[:, 2:3], wsc[:, 1:2], 1e-5, None, OP.max)
        VEC.drain()
        VEC.tensor_scalar(wsc[:, 3:4], wsc[:, 2:3], 0.5, None, OP.mult)
        VEC.tensor_scalar(wsc[:, 4:5], wsc[:, 2:3], -0.5, None, OP.mult)
        mi = VEC.drain()
        p.inc(mi, "V")
        w1mean_tok = p.tok("V")
        sA.close()
        hfm = ef(nc.sbuf_tensor("hfm", [128, 2056], F16))
        h2fm = ef(nc.sbuf_tensor("h2fm", [128, 2048], F32))
        h2st = ef(nc.sbuf_tensor("h2st", [128, 2048], F32))
        diag3 = ef(nc.sbuf_tensor("diag3", [128, 3, 128], F16))

        p.wait(VEC, cvw_tok)
        p.wait(VEC, w1mean_tok)
        mi = VEC.tensor_scalar(cvw[:, :, :], cvw[:, :, :], wsc[:, 2:3], None,
                               OP.mult)
        VEC.drain()
        p.inc(mi, "V")
        cvw_scaled = p.tok("V")

        f2_tok = [None] * nw1
        w1t_tok = [None] * nw1

        def f2_chunk(c, first=False):
            p.wait(VEC, ("w1c" if c % 2 == 0 else "w1d", 16 * (c // 2 + 1)))
            if first:
                p.wait(VEC, w1mean_tok)
            if c > 0:
                p.wait(VEC, p.tok("G"))
            VEC.tensor_scalar(m1b[:, :], wst[:, c % 2, :], wsc[:, 3:4], None,
                              OP.is_gt)
            mi = VEC.tensor_scalar(m2b[:, :], wst[:, c % 2, :], wsc[:, 4:5],
                                   None, OP.is_lt)
            VEC.drain()
            p.inc(mi, "V")
            f2_tok[c] = p.tok("V")
            if c + 2 < nw1:
                p.wait(SYN, f2_tok[c])
                p.inc(SYN.dma_start(wst[:, c % 2, :], w1_d[ts(c + 2), :]),
                      "w1c" if c % 2 == 0 else "w1d", 16)
            p.wait(GPS, f2_tok[c])
            if c > 0:
                p.wait(GPS, p.tok("T"))
            mi = GPS.tensor_tensor(ternb[:, :], m1b[:, :], m2b[:, :],
                                   OP.subtract)
            GPS.drain()
            p.inc(mi, "G")
            p.wait(TEN, p.tok("G"))
            for half in range(2):
                for j in range(4):
                    kch = half * 4 + j
                    mi = TEN.matmul(psC[:, half * 512 + j * 128:
                                        half * 512 + (j + 1) * 128],
                                    ternb[:, ts(kch)], identh[:, :],
                                    start=True, stop=True)
                p.inc(mi, "T")
            f2t = p.tok("T")
            p.wait(SCA, f2t)
            mi = SCA.copy(
                bass.AP(w1T, c * 128, [[32768, 128], [4096, 4], [1, 128]]),
                psC[:, 0:512])
            p.inc(mi, "S")
            p.wait(VEC, f2t)
            mi = VEC.tensor_copy(
                bass.AP(w1T, 4 * 4096 + c * 128,
                        [[32768, 128], [4096, 4], [1, 128]]),
                psC[:, 512:1024])
            VEC.drain()
            p.inc(mi, "V")
            w1t_tok[c] = (p.tok("V"), p.tok("S"))

        p.wait(SYN, w1mean_tok)
        for c in range(min(2, nw1)):
            p.inc(SYN.dma_start(wst[:, c % 2, :], w1_d[ts(c), :]),
                  "w1c" if c % 2 == 0 else "w1d", 16)

        p.wait(VEC, ("V", p.cnt["V"]))
        mi = VEC.memset(hfm[:, 0:4], 0.0)
        mi = VEC.memset(hfm[:, 2052:2056], 0.0)
        VEC.drain()
        p.inc(mi, "V")
        f2_chunk(0, first=True)
        spill_cnt = 0
        for ci in range(32):
            if ci == 0:
                p.wait(TEN, xqs_ready)
                p.wait(TEN, xqs_ready_s)
            p.wait(TEN, w1t_tok[ci][0])
            p.wait(TEN, w1t_tok[ci][1])
            if ci > 0:
                p.wait(TEN, hfm_tok)
            for k in range(8):
                for n in range(4):
                    mi = TEN.matmul(psA[:, n * 512:(n + 1) * 512],
                                    w1T[:, k, ts(ci)],
                                    bass.AP(qbs, (4 * n) * 1024 + k * 128,
                                            [[NT * 1024, 128], [1024, 4],
                                             [1, 128]]),
                                    start=(k == 0), stop=(k == 7))
            p.inc(mi, "T")
            p.wait(VEC, p.tok("T"))
            mi = VEC.tensor_copy(hfm[:, 4:2052], psA[:, :])
            VEC.drain()
            p.inc(mi, "V")
            hfm_tok = p.tok("V")
            if ci + 1 < 32:
                if ci > 0:
                    p.wait(TEN, h2_tok)
                f2_chunk(ci + 1)
            p.wait(VEC, cvw_scaled)
            for dd in range(3):
                mi = VEC.tensor_scalar(diag3[:, dd, :], identh[:, :],
                                       cvw[:, ci:ci + 1, dd], None, OP.mult)
            VEC.drain()
            p.inc(mi, "V")
            p.wait(TEN, p.tok("V"))
            p.wait(TEN, hfm_tok)
            if ci + 1 < 32:
                p.wait(TEN, w1t_tok[ci + 1][1])
            for tc in range(4):
                pst = psB if tc < 2 else psC
                off = (tc % 2) * 512
                for dd in range(3):
                    mi = TEN.matmul(pst[:, off:off + 512],
                                    diag3[:, dd, :],
                                    hfm[:, tc * 512 + 3 + dd:tc * 512 + 3 + dd + 512],
                                    start=(dd == 0), stop=(dd == 2))
            p.inc(mi, "T")
            conv_tok = p.tok("T")
            p.wait(SCA, conv_tok)
            SCA.activation(h2fm[:, 0:512], psB[:, 0:512], AF.Silu,
                           bias=cvb[:, ci:ci + 1])
            SCA.activation(h2fm[:, 512:1024], psB[:, 512:1024], AF.Silu,
                           bias=cvb[:, ci:ci + 1])
            SCA.activation(h2fm[:, 1024:1536], psC[:, 0:512], AF.Silu,
                           bias=cvb[:, ci:ci + 1])
            mi = SCA.activation(h2fm[:, 1536:2048], psC[:, 1024 - 512:1024],
                                AF.Silu, bias=cvb[:, ci:ci + 1])
            p.inc(mi, "S")
            h2_tok = p.tok("S")
            p.wait(TEN, h2_tok)
            if ci >= 1:
                pv = (ci - 1) % 2
                p.wait(VEC, ("spl_a" if pv == 0 else "spl_b",
                             16 * ((ci - 1) // 2 + 1)))
                p.wait(SCA, ("spl_a" if pv == 0 else "spl_b",
                             16 * ((ci - 1) // 2 + 1)))
            for w in range(2):
                po = w * 1024
                for j in range(8):
                    tt = w * 8 + j
                    mi = TEN.transpose(psA[:, po + j * 128:po + (j + 1) * 128],
                                       h2fm[:, ts(tt)], identf[:, :])
                p.inc(mi, "T")
                wtok = p.tok("T")
                p.wait(SCA, wtok)
                mi = SCA.copy(
                    bass.AP(h2st, w * 8 * 128, [[2048, 128], [1, 512]]),
                    psA[:, po:po + 512])
                p.inc(mi, "S")
                p.wait(VEC, wtok)
                mi = VEC.tensor_copy(
                    bass.AP(h2st, w * 8 * 128 + 512,
                            [[2048, 128], [1, 512]]),
                    psA[:, po + 512:po + 1024])
                VEC.drain()
                p.inc(mi, "V")
            p.wait(TEN, p.tok("V"))
            p.wait(TEN, p.tok("S"))
            p.wait(SYN, p.tok("V"))
            p.wait(SYN, p.tok("S"))
            dst = bass.AP(h2sp_d.tensor, ci * 128,
                          [[DH, 128], [128 * DH, 16], [1, 128]])
            srcst = bass.AP(h2st, 0,
                            [[2048, 128], [128, 16], [1, 128]])
            p.inc(SYN.dma_start(dst, srcst),
                  "spl_a" if ci % 2 == 0 else "spl_b", 16)
            spill_cnt += 1
            if ci == 0:
                for c0 in range(2):
                    ap2 = w2_d[ts(c0 // 4), :][:, (c0 % 4) * 1024:
                                               (c0 % 4 + 1) * 1024]
                    p.inc(SYN.dma_start(wstm[:, c0 % 2, :], ap2),
                          "w2a" if c0 % 2 == 0 else "w2b", 16)
            p.wait(VEC, ("w2a" if ci % 2 == 0 else "w2b",
                         16 * (ci // 2 + 1)))
            mi = VEC.tensor_reduce(absc2w[:, ci:ci + 1],
                                   wstm[:, ci % 2, :], AX, OP.add,
                                   apply_absolute_value=True)
            VEC.drain()
            p.inc(mi, "V")
            if ci + 2 < 32:
                ap2 = w2_d[ts((ci + 2) // 4), :][:, ((ci + 2) % 4) * 1024:
                                                 ((ci + 2) % 4 + 1) * 1024]
                p.wait(SYN, p.tok("V"))
                p.inc(SYN.dma_start(wstm[:, ci % 2, :], ap2),
                      "w2a" if ci % 2 == 0 else "w2b", 16)
        spill_tok_a = p.tok("spl_a")
        spill_tok_b = p.tok("spl_b")
        w2mean_sum = p.tok("V")

    with ExitStack() as fctx:
        ef = fctx.enter_context
        w2T = ef(nc.sbuf_tensor("w2T", [128, 32, 1024], F16))
        absc2 = ef(nc.sbuf_tensor("absc2", [128, 4], F32))
        wsc2 = ef(nc.sbuf_tensor("wsc2", [128, 8], F32))
        t2col = ef(nc.sbuf_tensor("t2col", [128, 8], F32))
        sW = fctx.enter_context(ExitStack())
        es = sW.enter_context
        wst2 = es(nc.sbuf_tensor("wst2", [128, 2, 2048], F32))
        m1c = es(nc.sbuf_tensor("m1c", [128, 2048], F32))
        m2c = es(nc.sbuf_tensor("m2c", [128, 2048], F32))
        tern2 = es(nc.sbuf_tensor("tern2", [128, 2048], F16))

        p.wait(SYN, spill_tok_a)
        p.wait(SYN, spill_tok_b)
        p.wait(VEC, spill_tok_a)
        p.wait(VEC, spill_tok_b)
        p.wait(VEC, w2mean_sum)
        mi = VEC.tensor_reduce(absc2[:, 1:2], absc2w[:, 0:32], AX, OP.add)
        VEC.drain()
        wmc = es(nc.sbuf_tensor("wmc", [128, 4], BF16))
        wrow2 = es(nc.sbuf_tensor("wrow2", [1, 4], BF16))
        mi = VEC.tensor_copy(wmc[:, 0:1], absc2[:, 1:2])
        VEC.drain()
        mi = VEC.scalar_tensor_tensor(wmc[:, 1:2], wmc[:, 0:1], -1.0,
                                      absc2[:, 1:2], OP.mult, OP.add)
        VEC.drain()
        p.inc(mi, "V")
        p.wait(TEN, p.tok("V"))
        mi = TEN.matmul(psC[0:1, 0:2], onecol[:, :], wmc[:, 0:2],
                        start=True, stop=True)
        p.inc(mi, "T")
        p.wait(VEC, p.tok("T"))
        mi = VEC.tensor_copy(absc2[:1, 2:4], psC[0:1, 0:2])
        VEC.drain()
        mi = VEC.tensor_tensor(absc2[:1, 3:4], absc2[:1, 2:3], absc2[:1, 3:4],
                               OP.add)
        VEC.drain()
        mi = VEC.tensor_copy(wrow2[0:1, 0:1], absc2[:1, 3:4])
        VEC.drain()
        mi = VEC.scalar_tensor_tensor(wrow2[0:1, 1:2], wrow2[0:1, 0:1], -1.0,
                                      absc2[:1, 3:4], OP.mult, OP.add)
        VEC.drain()
        p.inc(mi, "V")
        p.wait(TEN, p.tok("V"))
        for ii, pslice in enumerate((pat[0:1, 0:128], pat[0:1, 128:256])):
            mi = TEN.matmul(psC[:, 1:3], pslice, wrow2[0:1, 0:2],
                            start=(ii == 0), stop=True,
                            skip_group_check=True)
        p.inc(mi, "T")
        p.wait(VEC, p.tok("T"))
        mi = VEC.tensor_copy(wsc2[:, 6:8], psC[:, 1:3])
        VEC.drain()
        mi = VEC.tensor_tensor(wsc2[:, 0:1], wsc2[:, 6:7], wsc2[:, 7:8], OP.add)
        VEC.drain()
        VEC.tensor_scalar(wsc2[:, 1:2], wsc2[:, 0:1], 1.0 / (DH * D), None,
                          OP.mult)
        VEC.drain()
        VEC.tensor_scalar(wsc2[:, 2:3], wsc2[:, 1:2], 1e-5, None, OP.max)
        VEC.drain()
        VEC.tensor_scalar(wsc2[:, 3:4], wsc2[:, 2:3], 0.5, None, OP.mult)
        VEC.tensor_scalar(wsc2[:, 4:5], wsc2[:, 2:3], -0.5, None, OP.mult)
        mi = VEC.drain()
        p.inc(mi, "V")
        w2mean_tok = p.tok("V")

        g2_tok = [None] * 16
        p.wait(SYN, w2mean_tok)
        for c in range(2):
            ap = w2_d[ts(c // 2), :][:, (c % 2) * 2048:(c % 2 + 1) * 2048]
            p.inc(SYN.dma_start(wst2[:, c % 2, :], ap),
                  "w2c" if c % 2 == 0 else "w2d", 16)
        for c in range(16):
            p.wait(SCA, ("w2c" if c % 2 == 0 else "w2d", 16 * (c // 2 + 1)))
            if c > 0:
                p.wait(SCA, p.tok("G"))
            mi = SCA.activation(m1c[:, :], wst2[:, c % 2, :], AF.Abs)
            SCA.drain()
            mi = SCA.activation(m2c[:, :], wst2[:, c % 2, :], AF.Sign)
            SCA.drain()
            p.inc(mi, "S")
            g2_tok[c] = p.tok("S")
            if c + 2 < 16:
                ap = w2_d[ts((c + 2) // 2), :][:, ((c + 2) % 2) * 2048:
                                               ((c + 2) % 2 + 1) * 2048]
                p.wait(SYN, g2_tok[c])
                p.inc(SYN.dma_start(wst2[:, c % 2, :], ap),
                      "w2c" if c % 2 == 0 else "w2d", 16)
            p.wait(VEC, g2_tok[c])
            p.wait(VEC, w2mean_tok)
            mi = VEC.tensor_scalar(m1c[:, :], m1c[:, :], wsc2[:, 3:4], None,
                                   OP.is_gt)
            VEC.drain()
            p.inc(mi, "V")
            p.wait(GPS, p.tok("V"))
            if c > 0:
                p.wait(GPS, p.tok("T"))
            mi = GPS.tensor_tensor(tern2[:, :], m1c[:, :], m2c[:, :],
                                   OP.mult)
            GPS.drain()
            p.inc(mi, "G")
            p.wait(TEN, p.tok("G"))
            for w in range(2):
                po = w * 1024
                for j in range(8):
                    kc = w * 8 + j
                    mi = TEN.matmul(psA[:, po + j * 128:po + (j + 1) * 128],
                                    tern2[:, ts(kc)], identh[:, :],
                                    start=True, stop=True)
                p.inc(mi, "T")
                wtok = p.tok("T")
                p.wait(SCA, wtok)
                mi = SCA.copy(
                    bass.AP(w2T, ((c % 2) * 16 + w * 8) * 1024 + (c // 2) * 128,
                            [[32768, 128], [1024, 4], [1, 128]]),
                    psA[:, po:po + 512])
                p.inc(mi, "S")
                p.wait(VEC, wtok)
                mi = VEC.tensor_copy(
                    bass.AP(w2T,
                            ((c % 2) * 16 + w * 8 + 4) * 1024 + (c // 2) * 128,
                            [[32768, 128], [1024, 4], [1, 128]]),
                    psA[:, po + 512:po + 1024])
                VEC.drain()
                p.inc(mi, "V")
            p.wait(TEN, p.tok("V"))
            p.wait(TEN, p.tok("S"))
        w2T_ready = p.tok("V")
        w2T_ready_s = p.tok("S")
        sW.close()
        h2t = ef(nc.sbuf_tensor("h2t", [128, 2, 4096], F32))
        y2 = ef(nc.sbuf_tensor("y2", [128, 1024], F32))
        yi2 = ef(nc.sbuf_tensor("yi2", [128, 1024], I32))
        c2 = ef(nc.sbuf_tensor("c2", [128, 1024], F32))
        diag2 = ef(nc.sbuf_tensor("diag2", [128, 128], F16))

        def xq2i_ap(c0, c1):
            return bass.AP(qbs, c0, [[NT * 1024, 128], [1, c1 - c0]])

        def xq2f_ap(off, pat):
            return bass.AP(qbs, 4096 + off,
                           [[NT * 1024, 128]] + pat)

        p.wait(SYN, spill_tok_a)
        p.wait(SYN, spill_tok_b)
        p.wait(SYN, w2T_ready)
        p.wait(SYN, w2T_ready_s)
        p.wait(SYN, p.tok("G"))
        p.wait(SYN, p.tok("T"))
        p.inc(SYN.dma_start(h2t[:, 0, :], h2sp_d[ts(0), :]), "h2a", 16)
        b_tok = [None] * NT
        wave_tok = [None] * NT
        stok = [None] * NT
        w2mm_tok = [None] * NT
        vtok = [None] * NT

        def b_x1add(tt2):
            p.wait(VEC, w2mm_tok[tt2])
            mi = VEC.scalar_tensor_tensor(x1[:, tt2, :], psB[:, 0:1024],
                                          wsc2[:, 2:3], x1[:, tt2, :],
                                          OP.mult, OP.add)
            VEC.drain()
            p.inc(mi, "V")
            b_tok[tt2] = p.tok("V")

        for t in range(NT):
            tb = t % 2
            if t + 1 < NT:
                if t > 0:
                    p.wait(SYN, vtok[t - 1])
                p.inc(SYN.dma_start(h2t[:, 1 - tb, :], h2sp_d[ts(t + 1), :]),
                      "h2a" if (t + 1) % 2 == 0 else "h2b", 16)
            p.wait(VEC, ("h2a" if tb == 0 else "h2b", 16 * (t // 2 + 1)))
            ht = h2t[:, tb, :]
            mi = VEC.tensor_reduce(t2col[:, 0:1], ht, AX, OP.max,
                                   apply_absolute_value=True)
            VEC.drain()
            VEC.tensor_scalar(t2col[:, 1:2], t2col[:, 0:1], 1e-5, None, OP.max)
            VEC.drain()
            mi = VEC.reciprocal(t2col[:, 2:3], t2col[:, 1:2])
            VEC.drain()
            mi = VEC.tensor_scalar(t2col[:, 3:4], t2col[:, 2:3], 127.0, None,
                                   OP.mult)
            VEC.drain()
            VEC.reciprocal(t2col[:, 4:5], t2col[:, 3:4])
            VEC.drain()
            for hh in range(4):
                if t > 0 and hh % 2 == 0:
                    p.wait(VEC, wave_tok[t - 1][hh + 1])
                hs = ht[:, hh * 1024:(hh + 1) * 1024]
                VEC.tensor_scalar(y2[:, :], hs, t2col[:, 3:4], C_MAGIC,
                                  OP.mult, OP.add)
                VEC.drain()
                VEC.tensor_copy(yi2[:, :], y2[:, :])
                VEC.drain()
                VEC.tensor_tensor(y2[:, :], y2[:, :], yi2[:, :], OP.subtract)
                VEC.drain()
                VEC.tensor_scalar(c2[:, :], y2[:, :], 0.5, None, OP.is_ge)
                VEC.drain()
                VEC.scalar_tensor_tensor(c2[:, :], y2[:, :], -0.5, c2[:, :],
                                         OP.is_gt, OP.add)
                VEC.drain()
                VEC.scalar_tensor_tensor(xq2i_ap(hh * 1024, (hh + 1) * 1024),
                                         yi2[:, :], C_MAGIC + 1.0, c2[:, :],
                                         OP.subtract, OP.add)
                VEC.drain()
            mi = VEC.tensor_scalar(diag2[:, :], identh[:, :], t2col[:, 4:5],
                                   None, OP.mult)
            VEC.drain()
            p.inc(mi, "V")
            vtok[t] = p.tok("V")
            if DEBUG_DUMPS:
                p.wait(GPS, vtok[t])
                dmi = GPS.dma_start(dbg_d[ts(t), :], xq2i_ap(0, 4096))
                p.inc(dmi, "dbg", 16)
                p.wait(VEC, ("dbg", 16 * (t + 1)))
            if t > 0:
                b_x1add(t - 1)

            p.wait(TEN, vtok[t])
            if t > 0:
                p.wait(TEN, stok[t - 1])
            wave_tok[t] = []
            for w in range(4):
                po = (w % 2) * 1024
                if w >= 2:
                    p.wait(TEN, ("S", p.cnt["S"]))
                for j in range(8):
                    kc = w * 8 + j
                    mi = TEN.matmul(psA[:, po + j * 128:po + (j + 1) * 128],
                                    xq2i_ap(kc * 128, (kc + 1) * 128),
                                    diag2[:, :], start=True, stop=True)
                p.inc(mi, "T")
                wave_tok[t].append(p.tok("T"))
                p.wait(SCA, wave_tok[t][w])
                SCA.copy(
                    xq2f_ap((w * 8) * 128, [[1, 512]]),
                    psA[:, po:po + 512])
                mi = SCA.copy(
                    xq2f_ap((w * 8) * 128 + 512, [[1, 512]]),
                    psA[:, po + 512:po + 1024])
                SCA.drain()
                p.inc(mi, "S")
            stok[t] = p.tok("S")
            p.wait(TEN, stok[t])
            if t > 0:
                p.wait(TEN, b_tok[t - 1])
            p.wait(TEN, w2T_ready)
            p.wait(TEN, w2T_ready_s)
            for kc in range(32):
                for n in range(2):
                    mi = TEN.matmul(psB[:, n * 512:(n + 1) * 512],
                                    xq2f_ap(kc * 128, [[1, 128]]),
                                    w2T[:, kc, n * 512:(n + 1) * 512],
                                    start=(kc == 0), stop=(kc == 31))
            p.inc(mi, "T")
            w2mm_tok[t] = p.tok("T")
        b_x1add(NT - 1)
        if DEBUG_DUMPS:
            p.wait_now(GPS, "dbg")


def build_l1(nc, p, ctx, din, x1, qbs, psA, psB, psC, ident, identf, identh,
             onecol, pat, negm, cpool, iw_d, xis_d, gpm_d, sim_d, gates_d,
             qkvp_d, opj_d, out_d):
    import math
    TEN, VEC, SCA, GPS, SYN = nc.tensor, nc.vector, nc.scalar, nc.gpsimd, nc.sync
    e = ctx.enter_context
    LOG32 = math.log(32.0)

    l1_start = p.tok("V")
    for eng in (TEN, SCA, GPS, SYN):
        p.wait(eng, l1_start)

    with ExitStack() as lctx:
        el = lctx.enter_context
        mu1 = el(nc.sbuf_tensor("mu1", [128, NT], F32))
        var1 = el(nc.sbuf_tensor("var1", [128, NT], F32))
        sd1 = el(nc.sbuf_tensor("sd1", [128, NT], F32))
        isd1 = el(nc.sbuf_tensor("isd1", [128, NT], F32))
        nsqc = el(nc.sbuf_tensor("nsqc", [128, NT], F32))
        sdv = el(nc.sbuf_tensor("sdv", [128, NT], F32))
        ixn = el(nc.sbuf_tensor("ixn", [128, NT], F32))
        nsqh = el(nc.sbuf_tensor("nsqh", [128, 2], F16))
        st6 = el(nc.sbuf_tensor("st6", [128, 12], F32))
        st2b = el(nc.sbuf_tensor("st2b", [128, 2], F32))
        sc1 = el(nc.sbuf_tensor("sc1", [128, 8], F32))
        xntm1 = el(nc.sbuf_tensor("xntm1", [128, D], F16))
        nrowh = el(nc.sbuf_tensor("nrowh", [1, 2048], F16))
        nrowl = el(nc.sbuf_tensor("nrowl", [1, 2048], F16))
        simf = el(nc.sbuf_tensor("simf", [128, 8, 16], F16))
        sim2 = el(nc.sbuf_tensor("sim2", [128, 16], F16))
        simn = el(nc.sbuf_tensor("simn", [1, 64], F32))
        rtile = el(nc.sbuf_tensor("rtile", [128, 16], F32))
        gtile = el(nc.sbuf_tensor("gtile", [128, 16], F32))
        growb = el(nc.sbuf_tensor("growb", [1, 32], BF16))

        for k in range(8):
            p.inc(GPS.dma_start(simf[:, k, :], sim_d[ts(k), :]), "g_sim", 16)
        sim_tok = p.tok("g_sim")

        for t in range(NT):
            VEC.bn_stats(st6[:, 0:6], x1[:, t, 0:512])
            mi = VEC.bn_stats(st6[:, 6:12], x1[:, t, 512:1024])
            VEC.drain()
            mi = VEC.bn_aggr(st2b[:, :], st6[:, :])
            VEC.drain()
            VEC.tensor_copy(mu1[:, t:t + 1], st2b[:, 0:1])
            mi = VEC.tensor_copy(var1[:, t:t + 1], st2b[:, 1:2])
            VEC.drain()
        p.wait(VEC, sim_tok)
        for k in range(8):
            mi = VEC.tensor_tensor(sim2[:, :], simf[:, k, :], simf[:, k, :],
                                   OP.mult)
            VEC.drain()
            p.inc(mi, "V")
            p.wait(TEN, p.tok("V"))
            mi = TEN.matmul(psC[0:1, 0:16], onecol[:, :], sim2[:, :],
                            start=(k == 0), stop=(k == 7),
                            skip_group_check=True)
            p.inc(mi, "T")
            if k < 7:
                p.wait(VEC, p.tok("T"))
        p.wait(VEC, p.tok("T"))
        mi = VEC.tensor_copy(simn[0:1, 0:16], psC[0:1, 0:16])
        VEC.drain()
        p.inc(mi, "V")

        p.wait(SCA, p.tok("V"))
        mi = SCA.activation(sd1[:, :], var1[:, :], AF.Sqrt, bias=cpool[:, 0:1])
        p.inc(mi, "S")
        mi = SCA.activation(sdv[:, :], var1[:, :], AF.Sqrt)
        p.inc(mi, "S")
        p.wait(SCA, p.tok("S"))
        mi = SCA.activation(simn[0:1, 16:32], simn[0:1, 0:16], AF.Sqrt)
        p.inc(mi, "S")
        p.wait(VEC, p.tok("S"))
        mi = VEC.reciprocal(isd1[:, :], sd1[:, :])
        VEC.drain()
        VEC.tensor_tensor(nsqc[:, :], var1[:, :], isd1[:, :], OP.mult)
        VEC.drain()
        VEC.tensor_tensor(nsqc[:, :], nsqc[:, :], isd1[:, :], OP.mult)
        VEC.drain()
        VEC.tensor_scalar(nsqc[:, :], nsqc[:, :], 1024.0, None, OP.mult)
        VEC.drain()
        VEC.tensor_tensor(ixn[:, :], sdv[:, :], isd1[:, :], OP.mult)
        VEC.drain()
        VEC.tensor_scalar(ixn[:, :], ixn[:, :], 32.0, None, OP.mult)
        VEC.drain()
        VEC.tensor_scalar(ixn[:, :], ixn[:, :], 1e-12, None, OP.max)
        VEC.drain()
        mi = VEC.reciprocal(ixn[:, :], ixn[:, :])
        VEC.drain()
        VEC.tensor_scalar(simn[0:1, 32:48], simn[0:1, 16:32], 1e-12, None,
                          OP.max)
        VEC.drain()
        mi = VEC.reciprocal(simn[0:1, 48:64], simn[0:1, 32:48])
        VEC.drain()
        p.inc(mi, "V")

        prevtr = None
        for t in range(NT):
            VEC.tensor_tensor(sc1[:, 0:1], mu1[:, t:t + 1], isd1[:, t:t + 1],
                              OP.mult)
            VEC.drain()
            VEC.tensor_scalar(sc1[:, 1:2], sc1[:, 0:1], -1.0, None, OP.mult)
            VEC.drain()
            mi = VEC.tensor_scalar(xntm1[:, :], x1[:, t, :], isd1[:, t:t + 1],
                                   sc1[:, 1:2], OP.mult, OP.add)
            VEC.drain()
            VEC.tensor_copy(nsqh[:, 0:1], nsqc[:, t:t + 1])
            VEC.drain()
            mi = VEC.scalar_tensor_tensor(nsqh[:, 1:2], nsqh[:, 0:1], -1.0,
                                          nsqc[:, t:t + 1], OP.mult, OP.add)
            VEC.drain()
            p.inc(mi, "V")
            p.wait(TEN, p.tok("V"))
            if prevtr is not None:
                p.wait(TEN, prevtr)
            for half in range(2):
                for j in range(4):
                    kch = half * 4 + j
                    mi = TEN.matmul(psC[:, half * 512 + j * 128:
                                        half * 512 + (j + 1) * 128],
                                    xntm1[:, ts(kch)], identh[:, :],
                                    start=True, stop=True)
                p.inc(mi, "T")
            mi = TEN.matmul(psB[0:1, 0:128], nsqh[:, 0:1], identh[:, :],
                            start=True, stop=True)
            mi = TEN.matmul(psB[0:1, 128:256], nsqh[:, 1:2], identh[:, :],
                            start=True, stop=True)
            p.inc(mi, "T")
            trtok = p.tok("T")
            p.wait(VEC, trtok)
            for half in range(2):
                mi = VEC.tensor_copy(
                    bass.AP(qbs, half * 4 * 2048 + t * 128,
                            [[16384, 128], [2048, 4], [1, 128]]),
                    psC[:, half * 512:(half + 1) * 512])
            VEC.tensor_copy(nrowh[0:1, ts(t)], psB[0:1, 0:128])
            mi = VEC.tensor_copy(nrowl[0:1, ts(t)], psB[0:1, 128:256])
            VEC.drain()
            p.inc(mi, "V")
            prevtr = p.tok("V")
        xn1_ready = p.tok("V")

        with ExitStack() as actx:
            ea = actx.enter_context
            abgT = ea(nc.sbuf_tensor("abgT", [128, 8, 1024], F16))
            wts = ea(nc.sbuf_tensor("wts", [128, 8, 1024], BF16))
            xisc = ea(nc.sbuf_tensor("xisc", [128, 8], F32))
            xisr = ea(nc.sbuf_tensor("xisr", [1, 8, 128], F32))
            ahir = ea(nc.sbuf_tensor("ahir", [1, 8, 128], F16))
            alor = ea(nc.sbuf_tensor("alor", [1, 8, 128], F16))
            arow = ea(nc.sbuf_tensor("arow", [1, 3, 128], F32))
            biasw = ea(nc.sbuf_tensor("biasw", [128, 8], F32))
            swh = ea(nc.sbuf_tensor("swh", [1, 1024], F16))
            swl = ea(nc.sbuf_tensor("swl", [1, 1024], F16))
            swc = ea(nc.sbuf_tensor("swc", [128, 10], F32))
            ictx = actx.enter_context(ExitStack())
            ei = ictx.enter_context
            gbuf = ei(nc.sbuf_tensor("gbuf", [128, 2, 1024], F16))
            swsq = ei(nc.sbuf_tensor("swsq", [128, 1024], F32))
            iwf = ei(nc.sbuf_tensor("iwf", [128, 8, 1024], F16))

            p.inc(GPS.dma_start(xisc[:, :], bass.AP(xis_d.tensor, 0,
                   [[1, 128], [128, 8]]), allow_slow_non_contiguous=True),
                  "g_xis", 16)
            for r in range(8):
                p.inc(SYN.dma_start(xisr[0:1, r, :],
                                    bass.AP(xis_d.tensor, r * 128,
                                            [[128, 1], [1, 128]])), "s_xis", 16)
            p.wait(VEC, p.tok("g_xis"))
            p.wait(VEC, p.tok("s_xis"))
            mi = VEC.tensor_scalar(biasw[:, :], xisc[:, :], -LOG32, None,
                                   OP.add)
            VEC.drain()
            for r in range(8):
                VEC.tensor_tensor(arow[0:1, 0, :], xisr[0:1, r, :],
                                  xisr[0:1, r, :], OP.mult)
                VEC.drain()
                VEC.tensor_scalar(arow[0:1, 0, :], arow[0:1, 0, :], -0.5, None,
                                  OP.mult)
                VEC.drain()
                VEC.tensor_copy(ahir[0:1, r, :], arow[0:1, 0, :])
                VEC.drain()
                VEC.tensor_copy(arow[0:1, 1, :], ahir[0:1, r, :])
                VEC.drain()
                VEC.scalar_tensor_tensor(arow[0:1, 2, :], arow[0:1, 1, :],
                                         -1.0, arow[0:1, 0, :], OP.mult,
                                         OP.add)
                VEC.drain()
                mi = VEC.tensor_copy(alor[0:1, r, :], arow[0:1, 2, :])
                VEC.drain()
            p.inc(mi, "V")
            acorr_tok = p.tok("V")

            p.inc(GPS.dma_start(gbuf[:, 0, :], gpm_d[ts(0), :]), "g_gpa", 16)
            prevv = None
            for r in range(8):
                rb = r % 2
                if r + 1 < 8:
                    if prevv is not None:
                        p.wait(GPS, prevv)
                    p.inc(GPS.dma_start(gbuf[:, 1 - rb, :], gpm_d[ts(r + 1), :]),
                          "g_gpa" if (r + 1) % 2 == 0 else "g_gpb", 16)
                p.wait(VEC, ("g_gpa" if rb == 0 else "g_gpb",
                             16 * (r // 2 + 1)))
                mi = VEC.tensor_scalar(gbuf[:, rb, :], gbuf[:, rb, :],
                                       xisc[:, r:r + 1], None, OP.mult)
                VEC.drain()
                p.inc(mi, "V")
                prevv = p.tok("V")
                p.wait(TEN, prevv)
                if r > 0:
                    p.wait(TEN, p.tok("V"))
                for half in range(2):
                    for j in range(4):
                        kch = half * 4 + j
                        mi = TEN.matmul(psC[:, half * 512 + j * 128:
                                            half * 512 + (j + 1) * 128],
                                        gbuf[:, rb, ts(kch)], identh[:, :],
                                        start=True, stop=True)
                    p.inc(mi, "T")
                p.wait(VEC, p.tok("T"))
                for half in range(2):
                    mi = VEC.tensor_copy(
                        bass.AP(abgT, half * 4 * 1024 + r * 128,
                                [[8192, 128], [1024, 4], [1, 128]]),
                        psC[:, half * 512:(half + 1) * 512])
                VEC.drain()
                p.inc(mi, "V")
                prevv = p.tok("V")
            abgT_ready = p.tok("V")

            p.wait(GPS, abgT_ready)
            p.inc(GPS.dma_start(gbuf[:, 0, :], iw_d[ts(0), :]), "g_iwa", 16)
            prevv = None
            for r in range(8):
                rb = r % 2
                if r + 1 < 8:
                    if prevv is not None:
                        p.wait(GPS, prevv)
                    p.inc(GPS.dma_start(gbuf[:, 1 - rb, :], iw_d[ts(r + 1), :]),
                          "g_iwa" if (r + 1) % 2 == 0 else "g_iwb", 16)
                p.wait(VEC, ("g_iwa" if rb == 0 else "g_iwb",
                             16 * (r // 2 + 1)))
                mi = VEC.tensor_tensor(swsq[:, 0:1024], gbuf[:, rb, :],
                                       gbuf[:, rb, :], OP.mult)
                VEC.drain()
                mi = VEC.tensor_reduce(swc[:, r:r + 1], swsq[:, 0:1024], AX,
                                       OP.add)
                VEC.drain()
                p.inc(mi, "V")
                prevv = p.tok("V")
                p.wait(TEN, prevv)
                for half in range(2):
                    for j in range(4):
                        kch = half * 4 + j
                        mi = TEN.matmul(psC[:, half * 512 + j * 128:
                                            half * 512 + (j + 1) * 128],
                                        gbuf[:, rb, ts(kch)], identh[:, :],
                                        start=True, stop=True)
                    p.inc(mi, "T")
                p.wait(VEC, p.tok("T"))
                for half in range(2):
                    mi = VEC.tensor_copy(
                        bass.AP(iwf, half * 4 * 1024 + r * 128,
                                [[8192, 128], [1024, 4], [1, 128]]),
                        psC[:, half * 512:(half + 1) * 512])
                VEC.drain()
                p.inc(mi, "V")
                prevv = p.tok("V")
            for r in range(8):
                VEC.tensor_copy(nsqh[:, 0:1], swc[:, r:r + 1])
                VEC.drain()
                mi = VEC.scalar_tensor_tensor(nsqh[:, 1:2], nsqh[:, 0:1], -1.0,
                                              swc[:, r:r + 1], OP.mult, OP.add)
                VEC.drain()
                p.inc(mi, "V")
                p.wait(TEN, p.tok("V"))
                mi = TEN.matmul(psB[0:1, 0:128], nsqh[:, 0:1], identh[:, :],
                                start=True, stop=True)
                mi = TEN.matmul(psB[0:1, 128:256], nsqh[:, 1:2], identh[:, :],
                                start=True, stop=True)
                p.inc(mi, "T")
                p.wait(VEC, p.tok("T"))
                VEC.tensor_copy(swh[0:1, ts(r)], psB[0:1, 0:128])
                mi = VEC.tensor_copy(swl[0:1, ts(r)], psB[0:1, 128:256])
                VEC.drain()
                p.inc(mi, "V")
            iw_ready = p.tok("V")

            p.wait(TEN, abgT_ready)
            p.wait(TEN, iw_ready)
            p.wait(TEN, acorr_tok)
            p.wait(SCA, p.tok("V"))
            for r in range(8):
                for n in range(2):
                    for k in range(8):
                        mi = TEN.matmul(psA[:, n * 512:(n + 1) * 512],
                                        abgT[:, k, ts(r)],
                                        iwf[:, k, n * 512:(n + 1) * 512],
                                        start=(k == 0), stop=False)
                    for la, ra in ((ahir, swh), (ahir, swl),
                                   (alor, swh), (alor, swl)):
                        last = la is alor and ra is swl
                        mi = TEN.matmul(psA[:, n * 512:(n + 1) * 512],
                                        la[0:1, r, :],
                                        ra[0:1, n * 512:(n + 1) * 512],
                                        start=False, stop=last)
                p.inc(mi, "T")
                p.wait(SCA, p.tok("T"))
                if r > 0:
                    p.wait(SCA, p.tok("S"))
                SCA.activation(wts[:, r, 0:512], psA[:, 0:512], AF.Exp,
                               bias=biasw[:, r:r + 1])
                mi = SCA.activation(wts[:, r, 512:1024], psA[:, 512:1024],
                                    AF.Exp, bias=biasw[:, r:r + 1])
                p.inc(mi, "S")
                p.wait(TEN, p.tok("S"))
            wts_ready = p.tok("S")
            ictx.close()
            feats = ea(nc.sbuf_tensor("feats", [128, 8, 2048], BF16))

            p.wait(TEN, xn1_ready)
            for r in range(8):
                for tc in range(4):
                    pst = psA if tc % 2 == 0 else psB
                    for k in range(8):
                        mi = TEN.matmul(pst[:, 0:512],
                                        abgT[:, k, ts(r)],
                                        bass.AP(qbs,
                                                k * 2048 + tc * 512,
                                                [[NT * 1024, 128], [1, 512]]),
                                        start=(k == 0), stop=False)
                    for la, ra in ((ahir, nrowh), (ahir, nrowl),
                                   (alor, nrowh), (alor, nrowl)):
                        last = la is alor and ra is nrowl
                        mi = TEN.matmul(pst[:, 0:512], la[0:1, r, :],
                                        ra[0:1, tc * 512:(tc + 1) * 512],
                                        start=False, stop=last)
                    p.inc(mi, "T")
                    p.wait(SCA, p.tok("T"))
                    if not (r == 0 and tc == 0):
                        p.wait(SCA, p.tok("S"))
                    mi = SCA.activation(feats[:, r, tc * 512:(tc + 1) * 512],
                                        pst[:, 0:512], AF.Exp,
                                        bias=biasw[:, r:r + 1])
                    p.inc(mi, "S")
                    p.wait(TEN, p.tok("S"))
            feats_ready = p.tok("S")

            p.wait(TEN, feats_ready)
            for t in range(NT):
                if t > 0:
                    p.wait(TEN, p.tok("V"))
                for k in range(8):
                    for n in range(2):
                        mi = TEN.matmul(psB[:, n * 512:(n + 1) * 512],
                                        feats[:, k, ts(t)],
                                        wts[:, k, n * 512:(n + 1) * 512],
                                        start=(k == 0), stop=(k == 7))
                p.inc(mi, "T")
                p.wait(VEC, p.tok("T"))
                mi = VEC.tensor_tensor(x1[:, t, :], x1[:, t, :],
                                       psB[:, 0:1024], OP.add)
                VEC.drain()
                p.inc(mi, "V")
            at1_done = p.tok("V")

        build_l1ffn(nc, p, lctx, x1, qbs, simf, simn, psA, psB, psC,
                    identh, onecol, pat, negm, gates_d, qkvp_d, opj_d,
                    out_d, at1_done, ixn, rtile, gtile, growb)


def make_inputs(core_inputs):
    f32 = np.float32
    ci = core_inputs
    wsm = np.concatenate([ci["L0_attn_Wb"], ci["L0_attn_Wfd"], ci["L0_attn_Wsd"],
                          ci["L0_attn_Wfg"], ci["L0_attn_Wsg"]], axis=1)
    biasrow = np.zeros((1, 80), f32)
    biasrow[0, 16:32] = ci["L0_attn_fd_bias"]
    biasrow[0, 32:48] = ci["L0_attn_sd_bias"]
    import ml_dtypes
    ident = np.eye(128).astype(ml_dtypes.bfloat16)
    negm = (np.triu(np.ones((128, 128), f32), 1) * np.float32(-1e9)).astype(f32)
    qp, kp, vp = ci["L1_ffn_q_proj"], ci["L1_ffn_k_proj"], ci["L1_ffn_v_proj"]
    qkvp = np.zeros((D, 3 * NH * HD), f32)
    for ee in range(NH):
        qkvp[:, ee * 192:ee * 192 + 64] = qp[ee]
        qkvp[:, ee * 192 + 64:ee * 192 + 128] = kp[ee]
        qkvp[:, ee * 192 + 128:ee * 192 + 192] = vp[ee]
    base = {
        "d_wq": np.ascontiguousarray(ci["L0_attn_Wq"], f32),
        "d_wk": np.ascontiguousarray(ci["L0_attn_Wk"], f32),
        "d_wv": np.ascontiguousarray(ci["L0_attn_Wv"], f32),
        "d_wsm": np.ascontiguousarray(wsm, f32),
        "d_biasrow": biasrow,
        "d_wo": np.ascontiguousarray(ci["L0_attn_Wo"], f32),
        "d_w1": np.ascontiguousarray(ci["L0_ffn_w1"], f32),
        "d_convw": np.ascontiguousarray(ci["L0_ffn_conv_w"].reshape(DH, 3), f32),
        "d_convb": np.ascontiguousarray(ci["L0_ffn_conv_b"].reshape(DH, 1), f32),
        "d_w2": np.ascontiguousarray(ci["L0_ffn_w2"], f32),
        "d_iw": np.ascontiguousarray(ci["L1_attn_input_weights"], f32),
        "d_xis": np.ascontiguousarray(ci["L1_attn_xis"].reshape(D, 1), f32),
        "d_gpm": np.ascontiguousarray(ci["L1_attn_proj_matrix"], f32),
        "d_sim": np.ascontiguousarray(ci["L1_ffn_sim_matrix"], f32),
        "d_gates": np.ascontiguousarray(ci["L1_ffn_gates"].reshape(1, 16), f32),
        "d_qkvp": qkvp,
        "d_opj": np.ascontiguousarray(
            ci["L1_ffn_o_proj"].reshape(NH * HD, D), f32),
        "d_identI": ident,
        "d_negmI": negm,
    }
    x_full = np.ascontiguousarray(ci["L0_attn_hidden_states"], f32)
    in_maps = []
    for b in range(8):
        m = dict(base)
        m["d_x0"] = np.ascontiguousarray(x_full[b], f32)
        in_maps.append(m)
    return in_maps


def kernel(**inputs):
    nc, _ = build_program("full")
    in_maps = make_inputs(inputs)
    res = run_bass_kernel_spmd(nc, in_maps, core_ids=list(range(8)))
    out = np.stack([np.asarray(r["out"], np.float32) for r in res.results])
    return out


def build_l1ffn(nc, p, lctx, x1, qbs, simf, simn, psA, psB, psC,
                identh, onecol, pat, negm, gates_d, qkvp_d, opj_d,
                out_d, at1_done, ixn, rtile, gtile, growb):
    TEN, VEC, SCA, GPS, SYN = nc.tensor, nc.vector, nc.scalar, nc.gpsimd, nc.sync
    BIG = 1e30

    for eng in (TEN, SCA, GPS, SYN):
        p.wait(eng, at1_done)

    with ExitStack() as fctx:
        ef = fctx.enter_context
        qkvac = ef(nc.sbuf_tensor("qkvac", [128, NT, 192], F32))
        rwf = ef(nc.sbuf_tensor("rwf", [128, NT, 16], F32))
        qfm = ef(nc.sbuf_tensor("qfm", [128, 2048], F16))
        kfm = ef(nc.sbuf_tensor("kfm", [128, 2048], F16))
        vt3 = ef(nc.sbuf_tensor("vt3", [128, NT, 64], F16))
        lg = ef(nc.sbuf_tensor("lg", [128, 256], F32))
        lgb = ef(nc.sbuf_tensor("lgb", [128, 256], F32))
        lgc = ef(nc.sbuf_tensor("lgc", [128, 256], F32))
        lgd = ef(nc.sbuf_tensor("lgd", [128, 256], F32))
        gct = ef(nc.sbuf_tensor("gct", [128, 96], F32))
        qkh = ef(nc.sbuf_tensor("qkh", [128, 128], F16))
        grow = ef(nc.sbuf_tensor("grow", [1, 16], F32))
        with ExitStack() as sctx:
            es2 = sctx.enter_context
            slabq = es2(nc.sbuf_tensor("slabq", [128, 8, 3072], F16))
            zw = es2(nc.sbuf_tensor("zw", [128, 3072], F32))

            for k in range(8):
                p.inc(GPS.dma_start(slabq[:, k, :], qkvp_d[ts(k), :]),
                      "g_qkvp", 16)
            p.inc(SYN.dma_start(grow[0:1, :], gates_d), "s_gt", 16)
            slab_tok = p.tok("g_qkvp")

            p.wait(TEN, slab_tok)
            p.wait(TEN, p.tok("g_sim"))
            for t in range(NT):
                for k in range(8):
                    mi = TEN.matmul(psC[:, t * 16:(t + 1) * 16],
                                    bass.AP(qbs, k * 2048 + t * 128,
                                            [[NT * 1024, 128], [1, 128]]),
                                    simf[:, k, :],
                                    start=(k == 0), stop=(k == 7))
            p.inc(mi, "T")
            lg_tok = p.tok("T")

            p.wait(VEC, p.tok("s_gt"))
            p.wait(SCA, p.tok("s_gt"))
            mi = SCA.activation(simn[0:1, 0:16], grow[0:1, :], AF.Tanh,
                                scale=0.5)
            p.inc(mi, "S")
            p.wait(VEC, p.tok("S"))
            mi = VEC.tensor_scalar(simn[0:1, 0:16], simn[0:1, 0:16], 0.5, 0.5,
                                   OP.mult, OP.add)
            VEC.drain()
            VEC.tensor_copy(growb[0:1, 0:16], simn[0:1, 48:64])
            VEC.drain()
            mi = VEC.scalar_tensor_tensor(simn[0:1, 16:32], growb[0:1, 0:16],
                                          -1.0, simn[0:1, 48:64], OP.mult,
                                          OP.add)
            VEC.drain()
            mi = VEC.tensor_copy(growb[0:1, 16:32], simn[0:1, 16:32])
            VEC.drain()
            p.inc(mi, "V")
            p.wait(TEN, p.tok("V"))
            mi = TEN.matmul(psC[:, 256:272], pat[0:1, 0:128], growb[0:1, 0:16],
                            start=True, stop=False, skip_group_check=True)
            mi = TEN.matmul(psC[:, 256:272], pat[0:1, 128:256], growb[0:1, 0:16],
                            start=False, stop=False, skip_group_check=True)
            mi = TEN.matmul(psC[:, 256:272], pat[0:1, 0:128], growb[0:1, 16:32],
                            start=False, stop=False, skip_group_check=True)
            mi = TEN.matmul(psC[:, 256:272], pat[0:1, 128:256], growb[0:1, 16:32],
                            start=False, stop=True, skip_group_check=True)
            p.inc(mi, "T")
            p.wait(VEC, p.tok("T"))
            mi = VEC.tensor_copy(rtile[:, :], psC[:, 256:272])
            VEC.drain()
            p.inc(mi, "V")
            VEC.tensor_copy(growb[0:1, 0:16], simn[0:1, 0:16])
            VEC.drain()
            mi = VEC.scalar_tensor_tensor(simn[0:1, 16:32], growb[0:1, 0:16],
                                          -1.0, simn[0:1, 0:16], OP.mult,
                                          OP.add)
            VEC.drain()
            mi = VEC.tensor_copy(growb[0:1, 16:32], simn[0:1, 16:32])
            VEC.drain()
            p.inc(mi, "V")
            p.wait(TEN, p.tok("V"))
            mi = TEN.matmul(psC[:, 272:288], pat[0:1, 0:128], growb[0:1, 0:16],
                            start=True, stop=False, skip_group_check=True)
            mi = TEN.matmul(psC[:, 272:288], pat[0:1, 128:256], growb[0:1, 0:16],
                            start=False, stop=False, skip_group_check=True)
            mi = TEN.matmul(psC[:, 272:288], pat[0:1, 0:128], growb[0:1, 16:32],
                            start=False, stop=False, skip_group_check=True)
            mi = TEN.matmul(psC[:, 272:288], pat[0:1, 128:256], growb[0:1, 16:32],
                            start=False, stop=True, skip_group_check=True)
            p.inc(mi, "T")
            p.wait(VEC, p.tok("T"))
            mi = VEC.tensor_copy(gtile[:, :], psC[:, 272:288])
            VEC.drain()
            p.inc(mi, "V")
            gating_consts = p.tok("V")



            def bct(tens, j):
                return bass.AP(tens, j * 16, [[96, 128], [1, NT], [0, 16]])

            ixnb = bass.AP(ixn, 0, [[NT, 128], [1, NT], [0, 16]])
            rtb = bass.AP(rtile, 0, [[16, 128], [0, NT], [1, 16]])
            gtb = bass.AP(gtile, 0, [[16, 128], [0, NT], [1, 16]])
            lg3 = bass.AP(lg, 0, [[256, 128], [16, NT], [1, 16]])
            lgd3 = bass.AP(lgd, 0, [[256, 128], [16, NT], [1, 16]])
            lgb3 = bass.AP(lgb, 0, [[256, 128], [16, NT], [1, 16]])
            p.wait(VEC, lg_tok)
            VEC.tensor_tensor(lg[:, :], psC[:, 0:256], ixnb, OP.mult)
            VEC.drain()
            VEC.tensor_tensor(lg[:, :], lg[:, :], rtb, OP.mult)
            VEC.drain()
            mi = VEC.tensor_tensor(lg[:, :], lg[:, :], gtb, OP.subtract)
            VEC.drain()
            p.inc(mi, "V")
            VEC.tensor_scalar(lgb[:, :], lg[:, :], 0.0, None, OP.max)
            VEC.tensor_reduce(gct[:, 0:16], lg3, AX, OP.max)
            VEC.drain()
            VEC.tensor_tensor(lgd[:, :], lg[:, :], bct(gct, 0), OP.is_ge)
            VEC.drain()
            VEC.scalar_tensor_tensor(lgd[:, :], lgd[:, :], -BIG, lg[:, :],
                                     OP.mult, OP.add)
            VEC.drain()
            VEC.tensor_reduce(gct[:, 16:32], lgd3, AX, OP.max)
            VEC.drain()
            VEC.tensor_tensor(lgd[:, :], lg[:, :], bct(gct, 1), OP.is_ge)
            VEC.tensor_scalar(gct[:, 32:48], gct[:, 0:16], 0.0, None,
                              OP.is_gt)
            VEC.drain()
            VEC.tensor_scalar(lg[:, :], lg[:, :], 0.0, None, OP.is_gt)
            VEC.drain()
            VEC.tensor_tensor(lgc[:, :], lg[:, :], lgd[:, :], OP.subtract)
            VEC.drain()
            VEC.tensor_tensor(lgc[:, :], lgc[:, :], bct(gct, 2), OP.mult)
            VEC.drain()
            VEC.tensor_tensor(lg[:, :], lgc[:, :], lgd[:, :], OP.add)
            VEC.drain()
            VEC.tensor_tensor(lgd[:, :], lgb[:, :], lg[:, :], OP.mult)
            VEC.drain()
            VEC.tensor_reduce(gct[:, 48:64], lgd3, AX, OP.max)
            VEC.drain()
            mi = VEC.tensor_tensor(lgb[:, :], lgb[:, :], bct(gct, 3),
                                   OP.subtract)
            VEC.drain()
            p.inc(mi, "V")
            p.wait(SCA, p.tok("V"))
            mi = SCA.activation(lgb[:, :], lgb[:, :], AF.Exp)
            p.inc(mi, "S")
            p.wait(VEC, p.tok("S"))
            VEC.tensor_tensor(lgb[:, :], lgb[:, :], lg[:, :], OP.mult)
            VEC.drain()
            VEC.tensor_reduce(gct[:, 64:80], lgb3, AX, OP.add)
            VEC.drain()
            VEC.reciprocal(gct[:, 80:96], gct[:, 64:80])
            VEC.drain()
            mi = VEC.tensor_tensor(bass.AP(rwf, 0, [[256, 128], [1, 256]]),
                                   lgb[:, :], bct(gct, 5), OP.mult)
            VEC.drain()
            p.inc(mi, "V")
            rw_done = p.tok("V")

            prev_evac = None
            for t in range(NT):
                if prev_evac is not None:
                    p.wait(TEN, prev_evac)
                for k in range(8):
                    lhs = bass.AP(qbs, k * 2048 + t * 128,
                                  [[NT * 1024, 128], [1, 128]])
                    for n in range(4):
                        mi = TEN.matmul(psA[:, n * 512:(n + 1) * 512], lhs,
                                        slabq[:, k, n * 512:(n + 1) * 512],
                                        start=(k == 0), stop=(k == 7))
                    for n in range(2):
                        mi = TEN.matmul(psB[:, n * 512:(n + 1) * 512], lhs,
                                        slabq[:, k, 2048 + n * 512:
                                              2048 + (n + 1) * 512],
                                        start=(k == 0), stop=(k == 7))
                p.inc(mi, "T")
                z_tok = p.tok("T")

                p.wait(VEC, z_tok)
                if t == 0:
                    p.wait(VEC, rw_done)
                VEC.tensor_tensor(
                    bass.AP(zw, 0, [[3072, 128], [1, 10], [16, 192]]),
                    psA[:, 0:1920],
                    bass.AP(rwf, t * 16, [[NT * 16, 128], [1, 10], [0, 192]]),
                    OP.mult)
                VEC.tensor_scalar(
                    bass.AP(zw, 10, [[3072, 128], [16, 128]]),
                    psA[:, 1920:2048], rwf[:, t, 10:11], None, OP.mult)
                VEC.tensor_scalar(
                    bass.AP(zw, 128 * 16 + 10, [[3072, 128], [16, 64]]),
                    psB[:, 0:64], rwf[:, t, 10:11], None, OP.mult)
                mi = VEC.tensor_tensor(
                    bass.AP(zw, 11, [[3072, 128], [1, 5], [16, 192]]),
                    psB[:, 64:1024],
                    bass.AP(rwf, t * 16 + 11,
                            [[NT * 16, 128], [1, 5], [0, 192]]),
                    OP.mult)
                VEC.drain()
                p.inc(mi, "V")
                prev_evac = p.tok("V")
                mi = VEC.tensor_reduce(
                    qkvac[:, t, :],
                    bass.AP(zw, 0, [[3072, 128], [16, 192], [1, 16]]),
                    AX, OP.add)
                VEC.drain()
                p.inc(mi, "V")
                qkv_tok = p.tok("V")
                p.wait(SCA, qkv_tok)
                SCA.copy(qkh[:, 0:64], qkvac[:, t, 0:64])
                SCA.drain()
                SCA.copy(qkh[:, 64:128], qkvac[:, t, 64:128])
                SCA.drain()
                mi = SCA.copy(vt3[:, t, :], qkvac[:, t, 128:192])
                SCA.drain()
                p.inc(mi, "S")
                p.wait(TEN, p.tok("S"))
                mi = TEN.matmul(psC[0:64, 512:640], qkh[:, 0:64], identh[:, :],
                                start=True, stop=True)
                mi = TEN.matmul(psC[0:64, 640:768], qkh[:, 64:128],
                                identh[:, :], start=True, stop=True)
                p.inc(mi, "T")
                p.wait(VEC, p.tok("T"))
                VEC.tensor_copy(qfm[0:64, ts(t)], psC[0:64, 512:640])
                mi = VEC.tensor_copy(kfm[0:64, ts(t)], psC[0:64, 640:768])
                VEC.drain()
                p.inc(mi, "V")
            gating_done = p.tok("V")

        Pt = ef(nc.sbuf_tensor("Pt", [128, 2048], F16))
        PTg = ef(nc.sbuf_tensor("PTg", [128, 2048], F16))
        att = ef(nc.sbuf_tensor("att", [128, NT, 64], F16))
        zrow = ef(nc.sbuf_tensor("zrow", [128, 4], F32))
        aetm = ef(nc.sbuf_tensor("aetm", [128, 1024], F16))
        aefm = ef(nc.sbuf_tensor("aefm", [128, 8, 128], F16))
        opjf = ef(nc.sbuf_tensor("opjf", [128, 8, 1024], F16))

        p.wait(GPS, gating_done)
        for k in range(8):
            p.inc(GPS.dma_start(opjf[:, k, :], opj_d[ts(k), :]), "g_opj", 16)
        opj_tok = p.tok("g_opj")

        sc_tok = [None] * NT
        exp_tok = [None] * NT
        p.wait(TEN, gating_done)

        def emit_scores(qt):
            kend = (qt + 1) * 128
            nch = (kend + 511) // 512
            for n in range(nch):
                c0, c1 = n * 512, min((n + 1) * 512, kend)
                mi = TEN.matmul(psA[:, c0:c1], qfm[0:64, ts(qt)],
                                kfm[0:64, c0:c1], start=True, stop=True)
            p.inc(mi, "T")
            sc_tok[qt] = p.tok("T")

        emit_scores(0)
        for qt in range(NT):
            kend = (qt + 1) * 128
            p.wait(VEC, sc_tok[qt])
            if qt > 0:
                p.wait(VEC, p.tok("S"))
            mi = VEC.tensor_tensor(psA[:, kend - 128:kend],
                                   psA[:, kend - 128:kend], negm[:, :], OP.add)
            VEC.drain()
            mi = VEC.tensor_reduce(zrow[:, 0:1], psA[:, 0:kend], AX, OP.max)
            VEC.drain()
            mi = VEC.tensor_scalar(zrow[:, 1:2], zrow[:, 0:1], -0.125, None,
                                   OP.mult)
            VEC.drain()
            p.inc(mi, "V")
            p.wait(SCA, p.tok("V"))
            p.wait(SCA, p.tok("T"))
            mi = SCA.activation(Pt[:, 0:kend], psA[:, 0:kend], AF.Exp,
                                scale=0.125, bias=zrow[:, 1:2],
                                accum_out=zrow[:, 2:3])
            p.inc(mi, "S")
            exp_tok[qt] = p.tok("S")
            p.wait(VEC, exp_tok[qt])
            mi = VEC.reciprocal(zrow[:, 3:4], zrow[:, 2:3])
            VEC.drain()
            p.inc(mi, "V")
            p.wait(TEN, exp_tok[qt])
            if qt + 1 < NT:
                emit_scores(qt + 1)
            nkt = qt + 1
            pvi = 0
            for g in range(0, nkt, 8):
                nb = min(8, nkt - g)
                if g >= 8:
                    p.wait(TEN, p.tok("V"))
                for j in range(nb):
                    mi = TEN.matmul(psB[:, j * 128:(j + 1) * 128],
                                    Pt[:, ts(g + j)], identh[:, :],
                                    start=True, stop=True)
                p.inc(mi, "T")
                p.wait(VEC, p.tok("T"))
                mi = VEC.tensor_copy(PTg[:, 0:nb * 128], psB[:, 0:nb * 128])
                VEC.drain()
                p.inc(mi, "V")
                p.wait(TEN, p.tok("V"))
                for j in range(nb):
                    mi = TEN.matmul(psC[:, 0:64], PTg[:, ts(j)],
                                    vt3[:, g + j, :],
                                    start=(pvi == 0), stop=(g + j == qt),
                                    skip_group_check=True)
                    pvi += 1
                p.inc(mi, "T")
            p.wait(VEC, p.tok("T"))
            mi = VEC.tensor_scalar(att[:, qt, :], psC[:, 0:64], zrow[:, 3:4],
                                   None, OP.mult)
            VEC.drain()
            p.inc(mi, "V")
        attn_done = p.tok("V")

        p.wait(TEN, attn_done)
        p.wait(TEN, opj_tok)
        for t in range(NT):
            if t > 0:
                p.wait(VEC, p.tok("T"))
            mi = VEC.tensor_tensor(
                aetm[:, :],
                bass.AP(att, t * 64, [[NT * 64, 128], [0, 16], [1, 64]]),
                bass.AP(rwf, t * 16, [[NT * 16, 128], [1, 16], [0, 64]]),
                OP.mult)
            VEC.drain()
            p.inc(mi, "V")
            p.wait(TEN, p.tok("V"))
            for half in range(2):
                for j in range(4):
                    kch = half * 4 + j
                    mi = TEN.matmul(psC[:, half * 512 + j * 128:
                                        half * 512 + (j + 1) * 128],
                                    aetm[:, ts(kch)], identh[:, :],
                                    start=True, stop=True)
                p.inc(mi, "T")
            p.wait(VEC, p.tok("T"))
            for half in range(2):
                mi = VEC.tensor_copy(
                    bass.AP(aefm, half * 4 * 128, [[1024, 128], [1, 512]]),
                    psC[:, half * 512:(half + 1) * 512])
            VEC.drain()
            p.inc(mi, "V")
            p.wait(TEN, p.tok("V"))
            for k in range(8):
                for n in range(2):
                    mi = TEN.matmul(psB[:, n * 512:(n + 1) * 512],
                                    aefm[:, k, :],
                                    opjf[:, k, n * 512:(n + 1) * 512],
                                    start=(k == 0), stop=(k == 7))
            p.inc(mi, "T")
            p.wait(VEC, p.tok("T"))
            mi = VEC.tensor_tensor(x1[:, t, :], x1[:, t, :], psB[:, 0:1024],
                                   OP.add)
            VEC.drain()
            p.inc(mi, "V")
            p.wait(SYN, p.tok("V"))
            p.inc(SYN.dma_start(out_d[ts(t), :], x1[:, t, :]), "out", 16)
